# version 11
# baseline (speedup 1.0000x reference)
import numpy as np
import ml_dtypes
import concourse.bass as bass
import concourse.mybir as mybir
from concourse.ap import AP
from concourse.bass_utils import run_bass_kernel_spmd

F32 = mybir.dt.float32
BF16 = mybir.dt.bfloat16
AF = mybir.ActivationFunctionType
ALU = mybir.AluOpType
AX = mybir.AxisListType

P = 128
DM = 1024
S = 8192
H = 16
HD = 64
TLOC = 2048
NPAN_F = 16
NPAN_L = 4
NT_L = 16
FCH = 64
EPS_LN = 1e-5
EPS_ATT = 1e-3
EPS_RMS = 1e-6
L_FF = 64
NEXP = 8
DPAIRS = [(12, 0), (13, 0), (13, 1), (14, 0), (14, 1), (14, 2),
          (15, 0), (15, 1), (15, 2), (15, 3)]
WINW = 640
WROW = 656

ENGS = ("sync", "tensor", "vector", "scalar", "gpsimd")


class Sch:
    def __init__(self):
        self.items = {e: [] for e in ENGS}
        self.cnt = {}
        self.pend = {e: [] for e in ENGS}
        self.nbar = 0
        self.ds = "dS0"

    def set_ds(self, name):
        self.ds = name

    def _m(self, s):
        return self.ds if s == "dS" else s

    def c(self, s):
        return self.cnt.get(self._m(s), 0)

    def tok(self, s):
        s = self._m(s)
        return (s, self.cnt.get(s, 0))

    def add(self, eng, fn, waits=(), incs=()):
        w = [(self._m(s), v) for (s, v) in list(self.pend[eng]) + list(waits) if v > 0]
        self.pend[eng] = []
        ii = []
        for (s, n) in incs:
            s = self._m(s)
            self.cnt[s] = self.cnt.get(s, 0) + n
            ii.append((s, n))
        self.items[eng].append((w, fn, ii))

    def barrier(self):
        self.nbar += 1
        tgt = 5 * self.nbar
        for e in ENGS:
            self.add(e, None, incs=(("ph", 1),))
        for e in ENGS:
            self.pend[e].append(("ph", tgt))




class Scope:

    def __init__(self, nc):
        from contextlib import ExitStack
        self.nc = nc
        self.stk = ExitStack()

    def t(self, name, shape, dt):
        return self.stk.enter_context(self.nc.sbuf_tensor(name, shape, dt))

    def close(self):
        self.stk.close()


def build_program(debug=False):
    nc = bass.Bass("TRN2", target_bir_lowering=False)

    def din(name, shape, dt=F32):
        return nc.dram_tensor(name, list(shape), dt, kind="ExternalInput")

    x_b = din("x_b", (S, DM))
    x_loc = din("x_loc", (TLOC, DM))
    a_wq = din("a_wq", (DM, DM))
    a_wk = din("a_wk", (DM, DM))
    a_wv = din("a_wv", (2, DM, 512))
    a_gamma = din("a_gamma", (DM,))
    a_beta = din("a_beta", (DM,))
    relwin = din("relwin", (len(DPAIRS), P, WINW))
    winf128 = din("winf128", (P, 1))
    f_g1 = din("f_g1", (DM,))
    f_wq = din("f_wq", (DM, DM))
    f_bq = din("f_bq", (DM,))
    f_wdn = din("f_wdn", (DM, L_FF))
    f_bdn = din("f_bdn", (L_FF,))
    f_wup = din("f_wup", (P, 2 * DM))
    f_bup = din("f_bup", (2 * DM,))
    f_wo = din("f_wo", (DM, DM))
    f_bo = din("f_bo", (DM,))
    f_g2 = din("f_g2", (DM,))
    f_wg = din("f_wg", (DM, NEXP))
    f_bg = din("f_bg", (NEXP,))
    f_we = din("f_we", (NEXP, 2, DM, 512))
    gammab_d = din("gammab_d", (P, DM))
    betab_d = din("betab_d", (P, DM))
    g1b_d = din("g1b_d", (P, DM))
    g2b_d = din("g2b_d", (P, DM))
    bqb_d = din("bqb_d", (P, DM))
    bdnb_d = din("bdnb_d", (P, L_FF))
    bupb_d = din("bupb_d", (P, 2 * DM))
    bob_d = din("bob_d", (P, DM))
    bgb_d = din("bgb_d", (P, NEXP))
    identf_d = din("identf_d", (P, P))
    identb_d = din("identb_d", (P, P), BF16)

    y_out = nc.dram_tensor("y", [TLOC, DM], F32, kind="ExternalOutput")

    dkind = "ExternalOutput" if debug else "Internal"
    debug_v = (debug == "v")
    debug = (debug is True)
    if debug_v:
        dbg_wres = nc.dram_tensor("dbg_wres", [P, 8192], BF16, kind="ExternalOutput")
        dbg_vst = nc.dram_tensor("dbg_vst", [2, P, 8 * 4160], BF16, kind="ExternalOutput")
    xT_s = nc.dram_tensor("xT_s", [20, P, 4096], BF16, kind=dkind)
    k_s = nc.dram_tensor("k_s", [H, NPAN_F, 64, 512], BF16, kind=dkind)
    q_s = nc.dram_tensor("q_s", [H, NPAN_L, 64, 512], BF16, kind=dkind)
    v_s = nc.dram_tensor("v_s", [H, P, FCH * 65], BF16, kind=dkind)
    x1_s = nc.dram_tensor("x1_s", [TLOC, DM], F32, kind=dkind)
    v1_s = nc.dram_tensor("v1_s", [TLOC, DM], F32, kind=dkind)
    xe_s = nc.dram_tensor("xe_s", [NEXP, DM, DM], BF16, kind=dkind)
    if debug:
        dbg_outT = nc.dram_tensor("dbg_outT", [P, 8 * TLOC], F32, kind="ExternalOutput")
        dbg_den = nc.dram_tensor("dbg_den", [P, 256], F32, kind="ExternalOutput")
        dbg_delta = nc.dram_tensor("dbg_delta", [P, 2 * TLOC], F32, kind="ExternalOutput")
        dbg_cinf = nc.dram_tensor("dbg_cinf", [P, 16], F32, kind="ExternalOutput")
        dbg_st = nc.dram_tensor("dbg_st", [P, 64], F32, kind="ExternalOutput")
        dbg_comb = nc.dram_tensor("dbg_comb", [2, P, 8192], F32, kind="ExternalOutput")

    psA = nc.alloc_psum_tensor("psA", [P, 1024], F32)
    psB = nc.alloc_psum_tensor("psB", [P, 1024], F32)
    psC = nc.alloc_psum_tensor("psC", [P, 1024], F32)
    psD = nc.alloc_psum_tensor("psD", [P, 512], F32)
    psT = nc.alloc_psum_tensor("psT", [P, 1024], BF16)

    sch = Sch()

    identb = nc.alloc_sbuf_tensor("identb", [P, P], BF16)
    identf = nc.alloc_sbuf_tensor("identf", [P, P], F32)
    cst = nc.alloc_sbuf_tensor("cst", [P, 4], F32)
    ones1 = nc.alloc_sbuf_tensor("ones1", [1, P], F32)
    dbgst = nc.alloc_sbuf_tensor("dbgst", [P, 64], F32) if debug else None
    scope_AD = Scope(nc)
    den_nat = scope_AD.t("den_nat", [P, 256], F32)
    delta = scope_AD.t("delta", [P, 2 * TLOC], F32)
    cinf64 = scope_AD.t("cinf64", [P, 16], F32)

    def mm(out, lhsT, rhs, start, stop, tile_position=None):
        def f(eng, out=out, lhsT=lhsT, rhs=rhs, start=start, stop=stop, tp=tile_position):
            return eng.matmul(out, lhsT, rhs, start=start, stop=stop, tile_position=tp)
        return f

    def dma(out, in_):
        def f(eng, out=out, in_=in_):
            return eng.dma_start(out=out, in_=in_)
        return f

    def emit_stats(xap, n, eps, rstd_col, nmr_col, sums, sqscr, first_waits=()):
        eps = {EPS_LN: cst[:, 0:1], EPS_ATT: cst[:, 1:2], EPS_RMS: cst[:, 2:3]}[eps]
        inv_n = 1.0 / n
        sch.add("vector", lambda e, a=xap, o=sums: e.reduce_sum(o[:, 0:1], a, axis=AX.X),
                waits=first_waits)
        sch.add("vector", lambda e, a=xap, s=sqscr, o=sums: e.scalar_tensor_tensor(
            out=s[:, :], in0=a, scalar=1.0, in1=a, op0=ALU.mult, op1=ALU.mult,
            accum_out=o[:, 1:2]), incs=(("vV", 1),))
        w = sch.c("vV")
        sch.add("vector", lambda e, o=sums, k=inv_n: e.tensor_scalar(
            o[:, 2:3], o[:, 0:1], k, None, ALU.mult), waits=(("vV", w),))
        sch.add("vector", lambda e, o=sums, k=inv_n: e.tensor_scalar(
            o[:, 3:4], o[:, 1:2], k, None, ALU.mult), incs=(("vV", 1),))
        w = sch.c("vV")
        sch.add("vector", lambda e, o=sums: e.tensor_tensor(
            o[:, 4:5], o[:, 2:3], o[:, 2:3], ALU.mult),
            waits=(("vV", w),), incs=(("vV", 1),))
        w = sch.c("vV")
        sch.add("vector", lambda e, o=sums: e.tensor_tensor(
            o[:, 5:6], o[:, 3:4], o[:, 4:5], ALU.subtract),
            waits=(("vV", w),), incs=(("vV", 1),))
        v1 = sch.c("vV")
        sch.add("scalar", lambda e, o=sums, ep=eps: e.activation(
            o[:, 6:7], o[:, 5:6], AF.Sqrt, bias=ep, scale=1.0),
            waits=(("vV", v1),), incs=(("aA", 1),))
        a1 = sch.c("aA")
        sch.add("vector", lambda e, o=sums, r=rstd_col: e.reciprocal(r, o[:, 6:7]),
                waits=(("aA", a1),), incs=(("vV", 1),))
        w = sch.c("vV")
        sch.add("vector", lambda e, o=sums, r=rstd_col, m=nmr_col: e.scalar_tensor_tensor(
            out=m, in0=o[:, 2:3], scalar=-1.0, in1=r, op0=ALU.mult, op1=ALU.mult),
            waits=(("vV", w),), incs=(("vV", 1),))
        return sch.c("vV")

    with (
        nc.sbuf_tensor("xpan0", [P, 4096], F32) as xpan0,
        nc.sbuf_tensor("xpan1", [P, 4096], F32) as xpan1,
        nc.sbuf_tensor("sqscrA", [P, 1024], F32) as sqscrA,
        nc.sbuf_tensor("sumsA", [P, 8], F32) as sumsA,
        nc.sbuf_tensor("rstdA", [P, 1], F32) as rstdA,
        nc.sbuf_tensor("nmrA", [P, 1], F32) as nmrA,
        nc.sbuf_tensor("xtile0", [P, 1024], BF16) as xtile0,
        nc.sbuf_tensor("xtile1", [P, 1024], BF16) as xtile1,
        nc.sbuf_tensor("xtpA0", [P, 4096], BF16) as xtpA0,
        nc.sbuf_tensor("xtpA1", [P, 4096], BF16) as xtpA1,
    ):
        xpan = (xpan0, xpan1)
        xtile = (xtile0, xtile1)
        xtpA = (xtpA0, xtpA1)
        sch.add("sync", dma(identb[:, :], identb_d.ap()), incs=(("dC0", 16),))
        sch.add("sync", dma(identf[:, :], identf_d.ap()), incs=(("dC0", 16),))
        ident_ld = sch.c("dC0")
        sch.add("gpsimd", lambda e: e.memset(delta[:, :], 0.0), incs=(("gG", 1),))
        sch.add("vector", lambda e: e.memset(cst[:, 0:1], EPS_LN))
        sch.add("vector", lambda e: e.memset(cst[:, 1:2], EPS_ATT))
        sch.add("vector", lambda e: e.memset(cst[:, 2:3], EPS_RMS))
        sch.add("vector", lambda e: e.memset(ones1[:, :], 1.0), incs=(("vV", 1),))

        xtp_free_at = [0, 0]
        pan_apply_done = [0, 0]
        tile_tr_done = [0, 0]
        psT_evac_done = 0
        for pn in range(20):
            buf = pn % 2
            dX = f"dX{buf}"
            if pn < 16:
                src = AP(x_b, 512 * pn * DM, [[DM, P], [P * DM, 4], [1, DM]])
            else:
                src = AP(x_loc, 512 * (pn - 16) * DM, [[DM, P], [P * DM, 4], [1, DM]])
            sch.add("sync", dma(xpan[buf][:, :], src),
                    waits=(("vV", pan_apply_done[buf]),), incs=((dX, 16),))
            pld = (dX, sch.c(dX))
            for tt in range(4):
                xsl = xpan[buf][:, 1024 * tt:1024 * (tt + 1)]
                vw = emit_stats(xsl, DM, EPS_LN, rstdA[:, 0:1], nmrA[:, 0:1],
                                sumsA, sqscrA, first_waits=(pld,))
                tbuf = xtile[tt % 2]
                sch.add("vector", lambda e, a=xsl, t=tbuf: e.tensor_scalar(
                    t[:, :], a, rstdA[:, 0:1], nmrA[:, 0:1], ALU.mult, ALU.add),
                    waits=(("vV", vw), ("tT", tile_tr_done[tt % 2])),
                    incs=(("vV", 1),))
                vap = sch.c("vV")
                if debug and pn == 0:
                    sch.add("vector", lambda e, tt=tt: e.tensor_copy(
                        dbgst[:, 12 * tt:12 * tt + 8], sumsA[:, 0:8]),
                        waits=(("vV", vap),))
                    sch.add("vector", lambda e, tt=tt: e.tensor_copy(
                        dbgst[:, 12 * tt + 8:12 * tt + 9], rstdA[:, 0:1]))
                    sch.add("vector", lambda e, tt=tt: e.tensor_copy(
                        dbgst[:, 12 * tt + 9:12 * tt + 10], nmrA[:, 0:1]))
                for i in range(8):
                    w = []
                    if i == 0:
                        w = [("vV", vap), ("dC0", ident_ld), ("vV", psT_evac_done)]
                    sch.add("tensor", lambda e, t=tbuf, i=i: e.transpose(
                        psT[:, 128 * i:128 * (i + 1)], t[:, 128 * i:128 * (i + 1)],
                        identb[:, :]),
                        waits=w, incs=((("tT", 1),) if i == 7 else ()))
                tile_tr_done[tt % 2] = sch.c("tT")
                dst = xtpA[buf][:, :].rearrange("p (c t) -> p c t", c=8, t=512)[:, :, 128 * tt:128 * (tt + 1)]
                w = [("tT", sch.c("tT"))]
                if tt == 0:
                    w.append((f"dY{buf}", xtp_free_at[buf]))
                sch.add("vector", lambda e, d=dst: e.tensor_copy(
                    d, psT[:, :].rearrange("p (c t) -> p c t", c=8, t=128)),
                    waits=w, incs=(("vV", 1),))
                psT_evac_done = sch.c("vV")
            pan_apply_done[buf] = sch.c("vV")
            sch.add("sync", dma(xT_s.ap()[pn], xtpA[buf][:, :]),
                    waits=(("vV", sch.c("vV")),), incs=((f"dY{buf}", 16),))
            xtp_free_at[buf] = sch.c(f"dY{buf}")
        a1_store_toks = (("dY0", sch.c("dY0")), ("dY1", sch.c("dY1")))

    sch.barrier()

    with (
        nc.sbuf_tensor("wstage0", [P, 2048], F32) as wstage0,
        nc.sbuf_tensor("wstage1", [P, 2048], F32) as wstage1,
        nc.sbuf_tensor("wres", [P, 8192], BF16) as wres,
        nc.sbuf_tensor("xtpin0", [P, 4096], BF16) as xtpin0,
        nc.sbuf_tensor("xtpin1", [P, 4096], BF16) as xtpin1,
        nc.sbuf_tensor("pevac0", [P, 4096], BF16) as pevac0,
        nc.sbuf_tensor("pevac1", [P, 4096], BF16) as pevac1,
        nc.sbuf_tensor("vstage", [P, 8 * 4160], BF16) as vstage,
        nc.sbuf_tensor("winfb", [P, 1], BF16) as winfb,
        nc.sbuf_tensor("winff", [P, 1], F32) as winff,
        nc.sbuf_tensor("widef", [P, WINW], F32) as widef,
        nc.sbuf_tensor("wideb", [P, WINW], BF16) as wideb,
        nc.sbuf_tensor("crev", [P, 128], F32) as crev,
    ):
        wstage = (wstage0, wstage1)
        xtpin = (xtpin0, xtpin1)
        pevac = (pevac0, pevac1)
        wst_free = [0, 0]
        xtpin_free = [0, 0]
        pevac_free = [("dY0", 0), ("dY1", 0)]
        psab_free = [0, 0]

        def load_weight_cast(wdram, wbf, ncols, scale):
            for piece in range(0, 8, 2):
                bi = (piece // 2) % 2
                stg = wstage[bi]
                src = AP(wdram, piece * P * ncols,
                         [[ncols, P], [P * ncols, 2], [1, ncols]])
                sch.add("sync", dma(stg[:, 0:2 * ncols], src),
                        waits=(("vV", wst_free[bi]),), incs=((f"dW{bi}", 16),))
                dl = sch.c(f"dW{bi}")
                sch.add("vector", lambda e, s=stg, n=2 * ncols, d=wbf,
                        o=piece * ncols, sc=scale: e.tensor_scalar(
                            d[:, o:o + n], s[:, 0:n], sc, None, ALU.mult),
                        waits=((f"dW{bi}", dl),), incs=(("vV", 1),))
                wst_free[bi] = sch.c("vV")
            return sch.c("vV")

        def proj_sweep(panels, out_dram, npan, wready):
            for idx, pn in enumerate(panels):
                buf = idx % 2
                sch.add("sync", dma(xtpin[buf][:, :], xT_s.ap()[pn]),
                        waits=(("tT", xtpin_free[buf]),) + a1_store_toks,
                        incs=((f"dX{buf}", 16),))
                dl = sch.c(f"dX{buf}")
                for dcc in range(8):
                    ps = (psA, psB)[dcc % 2]
                    for ccc in range(8):
                        w = []
                        if ccc == 0:
                            w = [(f"dX{buf}", dl), ("vV", wready), ("aA", psab_free[dcc % 2])]
                        sch.add("tensor", mm(
                            ps[:, 0:512],
                            wres[:, ccc * 1024 + dcc * P: ccc * 1024 + (dcc + 1) * P],
                            xtpin[buf][:, 512 * ccc:512 * (ccc + 1)],
                            start=(ccc == 0), stop=(ccc == 7)),
                            waits=w, incs=((("tT", 1),) if ccc == 7 else ()))
                    tw = sch.c("tT")
                    w = [("tT", tw)]
                    if dcc == 0:
                        w.append(pevac_free[buf])
                    sch.add("scalar", lambda e, d=pevac[buf], o=dcc * 512, ps=ps: e.copy(
                        d[:, o:o + 512], ps[:, 0:512]),
                        waits=w, incs=(("aA", 1),))
                    psab_free[dcc % 2] = sch.c("aA")
                xtpin_free[buf] = sch.c("tT")
                blk = npan * 64 * 512
                for hh in range(2):
                    dst = AP(out_dram, idx * 64 * 512 + hh * blk,
                             [[512, 64], [2 * blk, 8], [1, 512]])
                    srcv = pevac[buf][64 * hh:64 * hh + 64, :].rearrange(
                        "p (c t) -> p c t", c=8, t=512)
                    sch.add("sync", dma(dst, srcv),
                            waits=(("aA", sch.c("aA")),) if hh == 0 else (),
                            incs=((f"dY{buf}", 16),))
                pevac_free[buf] = (f"dY{buf}", sch.c(f"dY{buf}"))

        wr = load_weight_cast(a_wk, wres, 1024, 1.0)
        proj_sweep(list(range(16)), k_s, NPAN_F, wr)
        sch.barrier()

        wr = load_weight_cast(a_wq, wres, 1024, 1.0 / 32.0)
        proj_sweep([16, 17, 18, 19], q_s, NPAN_L, wr)
        sch.barrier()

        sch.add("sync", dma(winff[:, :], winf128.ap()), incs=(("dC1", 16),))
        dwf = sch.c("dC1")
        sch.add("vector", lambda e: e.tensor_copy(winfb[:, :], winff[:, :]),
                waits=(("dC1", dwf),), incs=(("vV", 1),))
        winf_ready = sch.c("vV")

        vevac_done = [0, 0]
        cinf_evac = 0
        wide_mm_done = 0
        wide_cast_done = 0
        delta_add_done = 0
        vflush_prev = 0
        for gv in range(2):
            wres_free = sch.c("tT")
            for piece in range(0, 8, 2):
                bi = (piece // 2) % 2
                stg = wstage[bi]
                src = AP(a_wv, gv * DM * 512 + piece * P * 512,
                         [[512, P], [P * 512, 2], [1, 512]])
                sch.add("sync", dma(stg[:, 0:1024], src),
                        waits=(("vV", wst_free[bi]),), incs=((f"dW{bi}", 16),))
                dl = sch.c(f"dW{bi}")
                sch.add("vector", lambda e, s=stg, d=wres, o=piece * 512: e.tensor_scalar(
                    d[:, o:o + 1024], s[:, 0:1024], 1.0, None, ALU.mult),
                    waits=((f"dW{bi}", dl), ("tT", wres_free)), incs=(("vV", 1),))
                wst_free[bi] = sch.c("vV")
            wv_ready = sch.c("vV")
            sch.add("vector", lambda e: e.memset(
                vstage[:, :].rearrange("p (h c o) -> p h c o", h=8, c=64, o=65)[:, :, :, 64:65],
                1.0), waits=(("dF", vflush_prev), ("tT", wres_free)), incs=(("vV", 1),))
            ones_ready = sch.c("vV")

            for pn in range(16):
                buf = pn % 2
                sch.add("sync", dma(xtpin[buf][:, :], xT_s.ap()[pn]),
                        waits=(("tT", xtpin_free[buf]),), incs=((f"dX{buf}", 16),))
                dl = sch.c(f"dX{buf}")
                for tt in range(4):
                    ps = (psA, psB)[tt % 2]
                    for ccc in range(8):
                        w = []
                        if ccc == 0:
                            w = [(f"dX{buf}", dl), ("vV", wv_ready), ("vV", ones_ready),
                                 ("aA", vevac_done[tt % 2])]
                        sch.add("tensor", mm(
                            ps[:, 0:512],
                            xtpin[buf][:, 512 * ccc + 128 * tt: 512 * ccc + 128 * (tt + 1)],
                            wres[:, 512 * ccc:512 * (ccc + 1)],
                            start=(ccc == 0), stop=(ccc == 7)),
                            waits=w, incs=((("tT", 1),) if ccc == 7 else ()))
                    tw = sch.c("tT")
                    ct = pn * 4 + tt
                    dst = vstage[:, :].rearrange("p (h c o) -> p h c o",
                                                 h=8, c=64, o=65)[:, :, ct, 0:64]
                    sch.add("scalar", lambda e, d=dst, ps=ps: e.copy(
                        d, ps[:, 0:512].rearrange("p (h d) -> p h d", h=8, d=64)),
                        waits=(("tT", tw),), incs=(("aA", 1),))
                    vevac_done[tt % 2] = sch.c("aA")
                xtpin_free[buf] = sch.c("tT")
            vdone = sch.c("aA")

            if debug_v:
                sch.add("sync", dma(dbg_vst.ap()[gv], vstage[:, :]),
                        waits=(("aA", vdone),), incs=(("dF", 16),))
            for hg in range(8):
                sch.add("sync", dma(v_s.ap()[8 * gv + hg],
                                    vstage[:, 4160 * hg:4160 * (hg + 1)]),
                        waits=(("aA", vdone),), incs=(("dF", 16),))
            vflush_prev = sch.c("dF")

            for hg in range(8):
                h = 8 * gv + hg
                base = 64 * (h % 2)
                tp = (0, 64) if (h % 2) else (0, 0)
                for cb in range(FCH):
                    w = []
                    if cb == 0:
                        w = [("aA", vdone), ("vV", winf_ready), ("vV", cinf_evac)]
                    sch.add("tensor", mm(psC[base:base + 64, 0:1],
                                         vstage[:, 4160 * hg + 65 * cb:4160 * hg + 65 * cb + 64],
                                         winfb[:, :], start=(cb == 0), stop=(cb == FCH - 1),
                                         tile_position=tp),
                            waits=w, incs=((("tT", 1),) if cb == FCH - 1 else ()))
                tw = sch.c("tT")
                sch.add("vector", lambda e, b=base, h=h: e.tensor_copy(
                    cinf64[b:b + 64, h:h + 1], psC[b:b + 64, 0:1]),
                    waits=(("tT", tw),), incs=(("vV", 1),))
                cinf_evac = sch.c("vV")

            for pidx, (h, j) in enumerate(DPAIRS):
                if h // 8 != gv:
                    continue
                hg = h - 8 * gv
                base = 64 * (h % 2)
                tp = (0, 64) if (h % 2) else (0, 0)
                sch.add("sync", dma(widef[:, :], relwin.ap()[pidx]),
                        waits=(("vV", wide_cast_done),), incs=(("dR", 16),))
                dl = sch.c("dR")
                sch.add("vector", lambda e: e.tensor_copy(wideb[:, :], widef[:, :]),
                        waits=(("dR", dl), ("tT", wide_mm_done)),
                        incs=(("vV", 1),))
                wide_cast_done = sch.c("vV")
                for cb in range(FCH):
                    w = []
                    if cb == 0:
                        w = [("vV", wide_cast_done), ("vV", delta_add_done),
                             ("vV", cinf_evac)]
                    sch.add("tensor", mm(psC[base:base + 64, 128:256],
                                         vstage[:, 4160 * hg + 65 * cb:4160 * hg + 65 * cb + 64],
                                         wideb[:, 8 * cb:8 * cb + 128],
                                         start=(cb == 0), stop=(cb == FCH - 1),
                                         tile_position=tp),
                            waits=w, incs=((("tT", 1),) if cb == FCH - 1 else ()))
                wide_mm_done = sch.c("tT")
                sch.add("vector", lambda e, b=base: e.tensor_copy(
                    crev[b:b + 64, 0:128], psC[b:b + 64, 255:127:-1]),
                    waits=(("tT", wide_mm_done),), incs=(("vV", 1),))
                vv = sch.c("vV")
                sch.add("vector", lambda e, b=base, h=h: e.tensor_scalar(
                    crev[b:b + 64, 0:128], crev[b:b + 64, 0:128],
                    cinf64[b:b + 64, h:h + 1], None, ALU.subtract),
                    waits=(("vV", vv),), incs=(("vV", 1),))
                vv2 = sch.c("vV")
                cc6 = h // 2 - 6
                dv = delta[base:base + 64, :].rearrange(
                    "p (c t) -> p c t", c=2, t=TLOC)[:, cc6, j::16]
                sch.add("vector", lambda e, dv=dv, b=base: e.tensor_tensor(
                    dv, dv, crev[b:b + 64, 0:128], ALU.add),
                    waits=(("gG", 1), ("vV", vv2)), incs=(("vV", 1),))
                delta_add_done = sch.c("vV")
        a_done_toks = (("dY0", sch.c("dY0")), ("dY1", sch.c("dY1")),
                       ("dF", sch.c("dF")))

    sch.barrier()

    scope_BD = Scope(nc)
    outT = scope_BD.t("outT", [P, 8 * TLOC], F32)
    with (
        nc.sbuf_tensor("kdup0", [P, 8192], BF16) as kdup0,
        nc.sbuf_tensor("kdup1", [P, 8192], BF16) as kdup1,
        nc.sbuf_tensor("vbuf0", [P, 4160], BF16) as vbuf0,
        nc.sbuf_tensor("vbuf1", [P, 4160], BF16) as vbuf1,
        nc.sbuf_tensor("qdup0", [P, 2048], BF16) as qdup0,
        nc.sbuf_tensor("qdup1", [P, 2048], BF16) as qdup1,
        nc.sbuf_tensor("ebuf", [P, 2048], BF16) as ebuf,
        nc.sbuf_tensor("denrow", [P, 512], F32) as denrow,
        nc.sbuf_tensor("stage64", [P, 512], F32) as stage64,
    ):
        kdup = (kdup0, kdup1)
        vbuf = (vbuf0, vbuf1)
        qdup = (qdup0, qdup1)
        head_bufs_free = [0, 0]

        def load_head(h):
            buf = h % 2
            dH = f"dH{buf}"
            w = [("tT", head_bufs_free[buf])] + list(a_done_toks)
            for half in range(2):
                src = AP(k_s, h * (NPAN_F * 64 * 512),
                         [[512, 64], [64 * 512, 16], [1, 512]])
                sch.add("sync", dma(kdup[buf][64 * half:64 * half + 64, :], src),
                        waits=w, incs=((dH, 16),))
                w = []
            sch.add("sync", dma(vbuf[buf][:, :], v_s.ap()[h]), incs=((dH, 16),))
            for half in range(2):
                src = AP(q_s, h * (NPAN_L * 64 * 512),
                         [[512, 64], [64 * 512, 4], [1, 512]])
                sch.add("sync", dma(qdup[buf][64 * half:64 * half + 64, :], src),
                        incs=((dH, 16),))
            return (dH, sch.c(dH))

        head_ld = [None] * 17
        head_ld[0] = load_head(0)
        G = 0
        evd_count = [0, 0]
        dennat_done = 0
        stage64_free = 0
        for h in range(H):
            buf = h % 2
            if h + 1 < H:
                head_ld[h + 1] = load_head(h + 1)
            for pnl in range(NPAN_L):
                PNL = (h * NPAN_L + pnl) % 2
                evacc = psD[0:65, 0:512] if PNL == 0 else psC[0:65, 0:512]

                def emit_ev(pend, last):
                    half, pc0, pc1, Gv, first = pend
                    wev = [("ex", Gv + 1)]
                    if first:
                        wev.append(("vV", evd_count[PNL]))
                    for ci, c in enumerate((pc0, pc1)):
                        sch.add("tensor", mm(evacc,
                                             vbuf[buf][:, 65 * c:65 * c + 65],
                                             ebuf[:, half * 1024 + 512 * ci:
                                                  half * 1024 + 512 * (ci + 1)],
                                             start=(first and ci == 0),
                                             stop=(last and ci == 1)),
                                waits=(wev if ci == 0 else []),
                                incs=((("ev", 1),) if ci == 1 else ()))
                pend = None
                for k in range(32):
                    ps = (psA, psB)[G % 2]
                    c0, c1 = 2 * k, 2 * k + 1
                    w = []
                    if k == 0 and pnl == 0:
                        w.append(head_ld[h])
                    if G >= 2:
                        w.append(("ex", G - 1))
                    sch.add("tensor", mm(ps[:, 0:512],
                                         kdup[buf][0:64, 128 * c0:128 * (c0 + 1)],
                                         qdup[buf][0:64, 512 * pnl:512 * (pnl + 1)],
                                         start=True, stop=True, tile_position=(0, 0)),
                            waits=w)
                    sch.add("tensor", mm(ps[:, 512:1024],
                                         kdup[buf][64:128, 128 * c1:128 * (c1 + 1)],
                                         qdup[buf][64:128, 512 * pnl:512 * (pnl + 1)],
                                         start=True, stop=True, tile_position=(64, 0)),
                            incs=(("qk", 1),))
                    if pend is not None:
                        emit_ev(pend, last=False)
                    w = [("qk", G + 1)]
                    if G >= 2:
                        w.append(("ev", G - 1))
                    sch.add("scalar", lambda e, ps=ps, o=(G % 2) * 1024: e.activation(
                        ebuf[:, o:o + 1024], ps[:, :], AF.Exp),
                        waits=w, incs=(("ex", 1),))
                    pend = (G % 2, c0, c1, G, k == 0)
                    G += 1
                emit_ev(pend, last=True)
                evw = sch.c("ev")
                cc = h // 2
                colofs = TLOC * cc + 512 * pnl
                evtop = psD[0:64, 0:512] if PNL == 0 else psC[0:64, 0:512]
                evden = psD[64:65, 0:512] if PNL == 0 else psC[64:65, 0:512]
                if h % 2 == 0:
                    sch.add("vector", lambda e, o=colofs, s=evtop: e.tensor_copy(
                        outT[0:64, o:o + 512], s),
                        waits=(("ev", evw),), incs=(("vV", 1),))
                else:
                    sch.add("vector", lambda e, s=evtop: e.tensor_copy(
                        stage64[0:64, :], s),
                        waits=(("ev", evw), ("dB", stage64_free)), incs=(("vV", 1),))
                    vv = sch.c("vV")
                    sch.add("gpsimd", dma(outT[64:128, colofs:colofs + 512],
                                          stage64[0:64, :]),
                            waits=(("vV", vv),), incs=(("dB", 16),))
                    stage64_free = sch.c("dB")
                sch.add("vector", lambda e, s=evden: e.tensor_copy(
                    denrow[64:65, :], s), incs=(("vV", 1),))
                denw = sch.c("vV")
                for jj in range(4):
                    w = []
                    if jj == 0:
                        w = [("vV", denw), ("vV", dennat_done)]
                    sch.add("tensor", lambda e, jj=jj: e.transpose(
                        psC[:, 640 + jj:641 + jj], denrow[64:65, 128 * jj:128 * (jj + 1)],
                        identf[64:65, 64:65]),
                        waits=w, incs=((("tT", 1),) if jj == 3 else ()))
                tw = sch.c("tT")
                dst = den_nat[:, :].rearrange("p (t h) -> p t h", t=16, h=16)[:, 4 * pnl:4 * pnl + 4, h]
                sch.add("vector", lambda e, d=dst: e.tensor_copy(d, psC[:, 640:644]),
                        waits=(("tT", tw),), incs=(("vV", 1),))
                dennat_done = sch.c("vV")
                evd_count[PNL] = sch.c("vV")
            head_bufs_free[buf] = sch.c("tT")
        b_done_dG = sch.c("dB")

    sch.barrier()

    with (
        nc.sbuf_tensor("natx", [P, 1024], F32) as natx,
        nc.sbuf_tensor("dnat", [P, 256], F32) as dnat,
        nc.sbuf_tensor("rdn", [P, 16], F32) as rdn,
        nc.sbuf_tensor("gamb", [P, 1024], F32) as gamb,
        nc.sbuf_tensor("betb", [P, 1024], F32) as betb,
        nc.sbuf_tensor("cinfb", [P, 1024], F32) as cinfb,
        nc.sbuf_tensor("cirow", [32, 128], F32) as cirow,
        nc.sbuf_tensor("selrow", [1, 128], F32) as selrow,
        nc.sbuf_tensor("xres", [P, 1024], F32) as xres,
        nc.sbuf_tensor("x1t", [P, 1024], F32) as x1t,
        nc.sbuf_tensor("t3", [P, 1024], F32) as t3,
        nc.sbuf_tensor("sqscrD", [P, 1024], F32) as sqscrD,
        nc.sbuf_tensor("sumsD", [P, 8], F32) as sumsD,
        nc.sbuf_tensor("rstdD", [P, 1], F32) as rstdD,
        nc.sbuf_tensor("nmrD", [P, 1], F32) as nmrD,
    ):
        if debug:
            sch.add("sync", dma(dbg_st.ap(), dbgst[:, :]), incs=(("dDbg", 16),))
            sch.add("sync", dma(dbg_outT.ap(), outT[:, :]), incs=(("dDbg", 16),))
            sch.add("sync", dma(dbg_den.ap(), den_nat[:, :]), incs=(("dDbg", 16),))
            sch.add("sync", dma(dbg_delta.ap(), delta[:, :]), incs=(("dDbg", 16),))
            sch.add("sync", dma(dbg_cinf.ap(), cinf64[:, :]), incs=(("dDbg", 16),))
        sch.add("sync", dma(gamb[:, :], gammab_d.ap()), incs=(("dC1", 16),))
        sch.add("sync", dma(betb[:, :], betab_d.ap()), incs=(("dC1", 16),))
        gb_ld_d = sch.c("dC1")
        sch.add("tensor", lambda e: e.transpose(psC[0:16, 0:128], cinf64[:, :], identf[:, :]),
                incs=(("tT", 1),))
        tw = sch.c("tT")
        sch.add("vector", lambda e: e.memset(cirow[:, :], 0.0), incs=(("vV", 1),))
        mz = sch.c("vV")
        sch.add("vector", lambda e: e.tensor_copy(cirow[0:16, :], psC[0:16, 0:128]),
                waits=(("tT", tw), ("vV", mz)), incs=(("vV", 1),))
        cirow_w = sch.c("vV")
        row1 = cirow
        for h in range(H):
            sch.add("tensor", mm(psD[0:1, 0:128], identf[0:32, h:h + 1], cirow[:, :],
                                 start=True, stop=True),
                    waits=(([("vV", cirow_w), ("tT", tw)] if h == 0 else []) +
                           ([("vV", cinfb_evac)] if h > 0 else [])),
                    incs=(("tT", 1),))
            tw2 = sch.c("tT")
            sch.add("vector", lambda e, h=h: e.tensor_copy(selrow[0:1, :], psD[0:1, 0:128]),
                    waits=(("tT", tw2),), incs=(("vV", 1),))
            sw = sch.c("vV")
            sch.add("tensor", lambda e, h=h: e.matmul(
                psD[:, 128:192], ones1[:, :], selrow[0:1, 64 * (h % 2):64 * (h % 2) + 64],
                start=True, stop=True),
                waits=(("vV", sw),), incs=(("tT", 1),))
            tw3 = sch.c("tT")
            sch.add("vector", lambda e, h=h: e.tensor_copy(
                cinfb[:, 64 * h:64 * (h + 1)], psD[:, 128:192]),
                waits=(("tT", tw3),), incs=(("vV", 1),))
            cinfb_evac = sch.c("vV")
        bcast_ready_v = sch.c("vV")

        d_evac_done = 0
        xres_free = 0
        x1_st_done = 0
        for tt in range(NT_L):
            for i in range(8):
                w = []
                if i == 0:
                    w = [("vV", d_evac_done), ("dB", b_done_dG)]
                sch.add("tensor", lambda e, i=i, tt=tt: e.transpose(
                    psC[:, 128 * i:128 * (i + 1)],
                    outT[:, TLOC * i + 128 * tt:TLOC * i + 128 * (tt + 1)], identf[:, :]),
                    waits=w)
            for i in range(2):
                sch.add("tensor", lambda e, i=i, tt=tt: e.transpose(
                    psD[:, 128 * i:128 * (i + 1)],
                    delta[:, TLOC * i + 128 * tt:TLOC * i + 128 * (tt + 1)], identf[:, :]),
                    incs=((("tT", 1),) if i == 1 else ()))
            tw = sch.c("tT")
            sch.add("vector", lambda e: e.tensor_copy(natx[:, :], psC[:, 0:1024]),
                    waits=(("tT", tw),))
            sch.add("vector", lambda e: e.tensor_copy(dnat[:, :], psD[:, 0:256]),
                    incs=(("vV", 1),))
            d_evac_done = sch.c("vV")
            sch.add("vector", lambda e, tt=tt: e.reciprocal(
                rdn[:, :], den_nat[:, 16 * tt:16 * (tt + 1)]), incs=(("vV", 1),))
            vw = sch.c("vV")
            for hh in range(H):
                w = [("vV", vw)] if hh == 0 else []
                sch.add("vector", lambda e, hh=hh: e.tensor_scalar(
                    natx[:, 64 * hh:64 * (hh + 1)], natx[:, 64 * hh:64 * (hh + 1)],
                    rdn[:, hh:hh + 1], None, ALU.mult), waits=w)
            sch.add("vector", lambda e: e.tensor_tensor(
                natx[:, :], natx[:, :], cinfb[:, :], ALU.add),
                waits=(("vV", bcast_ready_v),))
            sch.add("vector", lambda e: e.tensor_tensor(
                natx[:, 768:1024], natx[:, 768:1024], dnat[:, :], ALU.add))
            vwst = emit_stats(natx[:, :], DM, EPS_ATT, rstdD[:, 0:1], nmrD[:, 0:1],
                              sumsD, sqscrD)
            sch.add("vector", lambda e: e.tensor_scalar(
                t3[:, :], natx[:, :], rstdD[:, 0:1], nmrD[:, 0:1], ALU.mult, ALU.add),
                waits=(("vV", vwst),))
            sch.add("vector", lambda e: e.tensor_tensor(t3[:, :], t3[:, :], gamb[:, :], ALU.mult),
                    waits=(("dC1", gb_ld_d),))
            sch.add("vector", lambda e: e.tensor_tensor(t3[:, :], t3[:, :], betb[:, :], ALU.add))
            vwst = emit_stats(t3[:, :], DM, EPS_LN, rstdD[:, 0:1], nmrD[:, 0:1],
                              sumsD, sqscrD)
            sch.add("vector", lambda e: e.tensor_scalar(
                t3[:, :], t3[:, :], rstdD[:, 0:1], nmrD[:, 0:1], ALU.mult, ALU.add),
                waits=(("vV", vwst),), incs=(("vV", 1),))
            src = AP(x_loc, 128 * tt * DM, [[DM, P], [1, DM]])
            sch.add("gpsimd", dma(xres[:, :], src),
                    waits=(("vV", xres_free),), incs=(("dG", 16),))
            dw = sch.c("dG")
            sch.add("vector", lambda e: e.tensor_tensor(x1t[:, :], t3[:, :], xres[:, :], ALU.add),
                    waits=(("dG", dw), ("dZ0", x1_st_done)), incs=(("vV", 1),))
            xres_free = sch.c("vV")
            vw = sch.c("vV")
            dst = AP(x1_s, 128 * tt * DM, [[DM, P], [1, DM]])
            sch.add("sync", dma(dst, x1t[:, :]), waits=(("vV", vw),), incs=(("dZ0", 16),))
            x1_st_done = sch.c("dZ0")
        d_x1_tok = ("dZ0", sch.c("dZ0"))
    scope_BD.close()
    scope_AD.close()

    sch.barrier()

    with (
        nc.sbuf_tensor("h2T", [P, 8 * TLOC], BF16) as h2T,
        nc.sbuf_tensor("gateT", [32, TLOC], F32) as gateT,
    ):
        scE13 = Scope(nc)
        hT = scE13.t("hT", [P, 8 * TLOC], BF16)
        wq2b = scE13.t("wq2b", [P, 8192], BF16)
        wdnb = scE13.t("wdnb", [P, 512], BF16)
        wupb = scE13.t("wupb", [64, 2048], BF16)
        wob = scE13.t("wob", [P, 8192], BF16)
        wgb = scE13.t("wgb", [P, 64], BF16)
        wstgE0 = scE13.t("wstgE0", [P, 2048], F32)
        wstgE1 = scE13.t("wstgE1", [P, 2048], F32)
        g1b = scE13.t("g1b", [P, 1024], F32)
        g2b = scE13.t("g2b", [P, 1024], F32)
        bqb = scE13.t("bqb", [P, 1024], F32)
        bdnb = scE13.t("bdnb", [P, 64], F32)
        bupb = scE13.t("bupb", [P, 2048], F32)
        bob = scE13.t("bob", [P, 1024], F32)
        bgb = scE13.t("bgb", [P, 8], F32)
        x1til0 = scE13.t("x1til0", [P, 1024], F32)
        x1til1 = scE13.t("x1til1", [P, 1024], F32)
        ut0 = scE13.t("ut0", [P, 1024], F32)
        ut1 = scE13.t("ut1", [P, 1024], F32)
        hb = scE13.t("hb", [P, 1024], BF16)
        h2b = scE13.t("h2b", [P, 1024], BF16)
        qt0 = scE13.t("qt0", [P, 1024], BF16)
        qt1 = scE13.t("qt1", [P, 1024], BF16)
        znat = scE13.t("znat", [P, 64], BF16)
        zT = scE13.t("zT", [64, 128], BF16)
        kvt0 = scE13.t("kvt0", [P, 2048], BF16)
        kvt1 = scE13.t("kvt1", [P, 2048], BF16)
        sumsG = scE13.t("sumsG", [P, 8], F32)
        attL = scE13.t("attL", [P, 256], F32)
        attM = scE13.t("attM", [P, 16], F32)
        attS = scE13.t("attS", [P, 16], F32)
        avacc = scE13.t("avacc", [P, 1024], F32)
        avtmp0 = scE13.t("avtmp0", [P, 1024], F32)
        avtmp1 = scE13.t("avtmp1", [P, 1024], F32)
        avb = scE13.t("avb", [P, 1024], BF16)
        avT = scE13.t("avT", [P, 1024], BF16)
        v1til = scE13.t("v1til", [P, 1024], F32)
        gtile = scE13.t("gtile", [P, 8], F32)
        sumsE = scE13.t("sumsE", [P, 8], F32)
        rstdE = scE13.t("rstdE", [P, 1], F32)
        r2E = scE13.t("r2E", [P, 1], F32)
        nmrE = scE13.t("nmrE", [P, 1], F32)
        if True:
            x1til = (x1til0, x1til1)
            wstgE = (wstgE0, wstgE1)
            wst_freeE = [0, 0]

            def loadw(wdram, wbf, ncols, scale, rows=DM):
                npc = max(1, 2048 // ncols)
                nchunk = rows // P
                for pi, piece in enumerate(range(0, nchunk, npc)):
                    bi = pi % 2
                    stg = wstgE[bi]
                    np_ = min(npc, nchunk - piece)
                    src = AP(wdram, piece * P * ncols,
                             [[ncols, P], [P * ncols, np_], [1, ncols]])
                    sch.add("sync", dma(stg[:, 0:np_ * ncols], src),
                            waits=(("vV", wst_freeE[bi]),), incs=((f"dW{bi}", 16),))
                    dl = sch.c(f"dW{bi}")
                    sch.add("vector", lambda e, s=stg, n=np_ * ncols, d=wbf,
                            o=piece * ncols, sc=scale, bi=bi: e.tensor_scalar(
                                d[:, o:o + n], s[:, 0:n], sc, None, ALU.mult),
                            waits=((f"dW{bi}", dl),), incs=(("vV", 1),))
                    wst_freeE[bi] = sch.c("vV")

            sch.add("vector", lambda e: e.memset(gateT[:, :], 0.0), incs=(("vV", 1),))
            loadw(f_wq, wq2b, DM, 0.125)
            loadw(f_wdn, wdnb, L_FF, 1.0)
            loadw(f_wo, wob, DM, 1.0)
            loadw(f_wg, wgb, NEXP, 1.0)
            sch.add("sync", dma(wstgE0[:, :], AP(f_wup, 0, [[2048, P], [1, 2048]])),
                    waits=(("vV", wst_freeE[0]),), incs=(("dW0", 16),))
            dl = sch.c("dW0")
            sch.add("vector", lambda e: e.tensor_scalar(
                wupb[:, :], wstgE0[0:64, 0:2048], 1.0, None, ALU.mult),
                waits=(("dW0", dl),), incs=(("vV", 1),))
            wst_freeE[0] = sch.c("vV")

            for (bt, bd) in ((g1b, g1b_d), (g2b, g2b_d), (bqb, bqb_d), (bdnb, bdnb_d),
                             (bupb, bupb_d), (bob, bob_d), (bgb, bgb_d)):
                sch.add("sync", dma(bt[:, :], bd.ap()), incs=(("dC1", 16),))
            wready = sch.c("vV")
            bias_ld = sch.c("dC1")

            x1_free = [0, 0]
            psab_free = [0, 0]
            psD_free = 0
            psT_free = 0
            v1_st_tok = ("dZ0", sch.c("dZ0"))
            v1_war = sch.c("dZ0")
            gate_done = 0
            avacc_free = 0
            avtmp_free = [0, 0]
            av_gps_done = sch.c("gG")
            utb = (ut0, ut1)
            qtb = (qt0, qt1)
            kvtb = (kvt0, kvt1)
            kv_readyP = [0, 0]

            def emit_pre(tt):
                nonlocal psT_free, psD_free, av_gps_done
                bi = tt % 2
                ut, qt, kvt = utb[bi], qtb[bi], kvtb[bi]
                src = AP(x1_s, 128 * tt * DM, [[DM, P], [1, DM]])
                w = [("vV", x1_free[bi]), d_x1_tok, ("vV", wready)]
                if debug and tt == 0:
                    w.append(("dDbg", sch.c("dDbg")))
                sch.add("sync", dma(x1til[bi][:, :], src),
                        waits=tuple(w), incs=((f"dX{bi}", 16),))
                x1l = sch.c(f"dX{bi}")
                vw = emit_stats(x1til[bi][:, :], DM, EPS_LN, rstdE[:, 0:1], nmrE[:, 0:1],
                                sumsE, avacc, first_waits=((f"dX{bi}", x1l),))
                sch.add("vector", lambda e, bi=bi, ut=ut: e.tensor_scalar(
                    ut[:, :], x1til[bi][:, :], rstdE[:, 0:1], nmrE[:, 0:1],
                    ALU.mult, ALU.add), waits=(("vV", vw),), incs=(("vV", 1),))
                x1_free[bi] = sch.c("vV")
                sch.add("vector", lambda e: e.tensor_tensor(
                    sumsE[:, 7:8], rstdE[:, 0:1], rstdE[:, 0:1], ALU.mult),
                    waits=(("vV", vw),), incs=(("vV", 1),))
                vw2 = sch.c("vV")
                sch.add("vector", lambda e: e.tensor_tensor(
                    sumsE[:, 7:8], sumsE[:, 7:8], sumsE[:, 5:6], ALU.mult),
                    waits=(("vV", vw2),), incs=(("vV", 1),))
                vw2 = sch.c("vV")
                sch.add("scalar", lambda e: e.activation(
                    sumsE[:, 7:8], sumsE[:, 7:8], AF.Sqrt, bias=cst[:, 2:3], scale=1.0),
                    waits=(("vV", vw2),), incs=(("aA", 1),))
                aw = sch.c("aA")
                sch.add("vector", lambda e: e.reciprocal(r2E[:, 0:1], sumsE[:, 7:8]),
                        waits=(("aA", aw),), incs=(("vV", 1),))
                vw3 = sch.c("vV")
                sch.add("vector", lambda e: e.tensor_scalar(
                    hb[:, :], ut[:, :], r2E[:, 0:1], None, ALU.mult),
                    waits=(("vV", vw3), ("dC1", bias_ld)))
                sch.add("vector", lambda e: e.tensor_tensor(
                    hb[:, :], hb[:, :], g1b[:, :], ALU.mult), incs=(("vV", 1),))
                hbw = sch.c("vV")
                for i in range(8):
                    w = [("vV", hbw), ("vV", psT_free)] if i == 0 else []
                    sch.add("tensor", lambda e, i=i: e.transpose(
                        psT[:, 128 * i:128 * (i + 1)], hb[:, 128 * i:128 * (i + 1)],
                        identb[:, :]),
                        waits=w, incs=((("tT", 1),) if i == 7 else ()))
                tw = sch.c("tT")
                dstT = hT[:, :].rearrange("p (c t) -> p c t", c=8, t=TLOC)[:, :, 128 * tt:128 * (tt + 1)]
                sch.add("vector", lambda e, d=dstT: e.tensor_copy(
                    d, psT[:, :].rearrange("p (c t) -> p c t", c=8, t=128)),
                    waits=(("tT", tw),), incs=(("vV", 1),))
                psT_free = sch.c("vV")
                hT_ready = sch.c("vV")

                for dh in range(2):
                    ps = (psA, psB)[dh]
                    for cc in range(8):
                        w = []
                        if cc == 0:
                            w = [("vV", hT_ready), ("vV", wready), ("vV", psab_free[dh])]
                        sch.add("tensor", mm(
                            ps[:, 0:512],
                            hT[:, TLOC * cc + 128 * tt:TLOC * cc + 128 * (tt + 1)],
                            wq2b[:, 1024 * cc + 512 * dh:1024 * cc + 512 * (dh + 1)],
                            start=(cc == 0), stop=(cc == 7)),
                            waits=w, incs=((("tT", 1),) if cc == 7 else ()))
                    tw = sch.c("tT")
                    sch.add("vector", lambda e, dh=dh, ps=ps: e.scalar_tensor_tensor(
                        out=qt[:, 512 * dh:512 * (dh + 1)], in0=bqb[:, 512 * dh:512 * (dh + 1)],
                        scalar=0.125, in1=ps[:, 0:512], op0=ALU.mult, op1=ALU.add),
                        waits=(("tT", tw),), incs=(("vV", 1),))
                    psab_free[dh] = sch.c("vV")
                for cc in range(8):
                    w = [("vV", psD_free), ("vV", hT_ready)] if cc == 0 else []
                    sch.add("tensor", mm(psD[:, 0:64],
                                         hT[:, TLOC * cc + 128 * tt:TLOC * cc + 128 * (tt + 1)],
                                         wdnb[:, 64 * cc:64 * (cc + 1)],
                                         start=(cc == 0), stop=(cc == 7)),
                            waits=w, incs=((("tT", 1),) if cc == 7 else ()))
                tw = sch.c("tT")
                sch.add("vector", lambda e: e.scalar_tensor_tensor(
                    out=znat[:, :], in0=bdnb[:, 0:64], scalar=1.0, in1=psD[:, 0:64],
                    op0=ALU.mult, op1=ALU.add),
                    waits=(("tT", tw),), incs=(("vV", 1),))
                zw = sch.c("vV")
                psD_free = sch.c("vV")
                sch.add("tensor", lambda e: e.transpose(
                    psT[0:64, 0:128], znat[:, :], identb[:, :]),
                    waits=(("vV", zw), ("vV", psT_free)), incs=(("tT", 1),))
                tw = sch.c("tT")
                sch.add("vector", lambda e: e.tensor_copy(zT[:, :], psT[0:64, 0:128]),
                        waits=(("tT", tw),), incs=(("vV", 1),))
                zTw = sch.c("vV")
                psT_free = sch.c("vV")
                for kk4 in range(4):
                    ps = (psA, psB)[kk4 % 2]
                    w = [("vV", psab_free[kk4 % 2])]
                    if kk4 == 0:
                        w.append(("vV", zTw))
                        w.append(("gG", av_gps_done))
                    sch.add("tensor", mm(ps[:, 0:512], zT[:, :],
                                         wupb[:, 512 * kk4:512 * (kk4 + 1)],
                                         start=True, stop=True),
                            waits=w, incs=(("tT", 1),))
                    tw = sch.c("tT")
                    sch.add("vector", lambda e, kk4=kk4, ps=ps: e.scalar_tensor_tensor(
                        out=kvt[:, 512 * kk4:512 * (kk4 + 1)],
                        in0=bupb[:, 512 * kk4:512 * (kk4 + 1)], scalar=1.0,
                        in1=ps[:, 0:512], op0=ALU.mult, op1=ALU.add),
                        waits=(("tT", tw),), incs=(("vV", 1),))
                    psab_free[kk4 % 2] = sch.c("vV")
                kv_ready = sch.c("vV")

                kv_readyP[bi] = sch.c("vV")

            def emit_post(tt):
                nonlocal psT_free, psD_free, av_gps_done, avacc_free, v1_war, v1_st_tok, gate_done
                bi = tt % 2
                ut, qt, kvt = utb[bi], qtb[bi], kvtb[bi]
                avtmp = (avtmp0, avtmp1)
                qr3 = qt[:, :].rearrange("p (h d) -> p h d", h=16, d=64)
                for gg in range(H):
                    tb = avtmp[gg % 2]
                    in1 = kvt[:, 128 * gg:128 * gg + 64].unsqueeze(1).broadcast_to(
                        [P, 16, 64])
                    w = [("vV", avtmp_free[gg % 2])]
                    if gg == 0:
                        w.append(("vV", kv_readyP[bi]))
                    sch.add("gpsimd", lambda e, tb=tb, in1=in1: e.tensor_tensor(
                        tb[:, :].rearrange("p (h d) -> p h d", h=16, d=64),
                        qr3, in1, ALU.mult),
                        waits=w, incs=(("gG", 1),))
                    gw = sch.c("gG")
                    dstc = attL[:, :].rearrange("p (h g) -> p h g", h=16, g=16)[:, :, gg]
                    sch.add("vector", lambda e, tb=tb, d=dstc: e.reduce_sum(
                        d, tb[:, :].rearrange("p (h d) -> p h d", h=16, d=64),
                        axis=AX.X),
                        waits=(("gG", gw),), incs=(("vV", 1),))
                    avtmp_free[gg % 2] = sch.c("vV")
                sch.add("vector", lambda e: e.reduce_max(
                    attM[:, :], attL[:, :].rearrange("p (h g) -> p h g", h=16, g=16),
                    axis=AX.X), waits=(("vV", sch.c("vV")),), incs=(("vV", 1),))
                sch.add("vector", None, waits=(("vV", sch.c("vV")),))
                sch.add("vector", lambda e: e.tensor_tensor(
                    attL[:, :].rearrange("p (h g) -> p h g", h=16, g=16),
                    attL[:, :].rearrange("p (h g) -> p h g", h=16, g=16),
                    attM[:, :].unsqueeze(2).broadcast_to([P, 16, 16]),
                    ALU.subtract), incs=(("vV", 1),))
                vw = sch.c("vV")
                sch.add("scalar", lambda e: e.activation(attL[:, :], attL[:, :], AF.Exp),
                        waits=(("vV", vw),), incs=(("aA", 1),))
                aw = sch.c("aA")
                sch.add("vector", lambda e: e.reduce_sum(
                    attS[:, :], attL[:, :].rearrange("p (h g) -> p h g", h=16, g=16),
                    axis=AX.X), waits=(("aA", aw),), incs=(("vV", 1),))
                sch.add("vector", None, waits=(("vV", sch.c("vV")),))
                sch.add("vector", lambda e: e.reciprocal(attS[:, :], attS[:, :]),
                        incs=(("vV", 1),))
                sch.add("vector", None, waits=(("vV", sch.c("vV")),))
                sch.add("vector", lambda e: e.tensor_tensor(
                    attL[:, :].rearrange("p (h g) -> p h g", h=16, g=16),
                    attL[:, :].rearrange("p (h g) -> p h g", h=16, g=16),
                    attS[:, :].unsqueeze(2).broadcast_to([P, 16, 16]),
                    ALU.mult), incs=(("vV", 1),))
                attw = sch.c("vV")
                for gg in range(H):
                    tb = avtmp[gg % 2]
                    in0 = attL[:, :].rearrange("p (h g) -> p h g", h=16, g=16)[
                        :, :, gg].unsqueeze(2).broadcast_to([P, 16, 64])
                    in1 = kvt[:, 128 * gg + 64:128 * (gg + 1)].unsqueeze(1).broadcast_to(
                        [P, 16, 64])
                    w = [("vV", avtmp_free[gg % 2])]
                    if gg == 0:
                        w.append(("vV", attw))
                    sch.add("gpsimd", lambda e, tb=tb, in0=in0, in1=in1: e.tensor_tensor(
                        tb[:, :].rearrange("p (h d) -> p h d", h=16, d=64),
                        in0, in1, ALU.mult),
                        waits=w, incs=(("gG", 1),))
                    gw = sch.c("gG")
                    if gg == 0:
                        sch.add("vector", lambda e, tb=tb: e.tensor_copy(
                            avacc[:, :], tb[:, :]),
                            waits=(("gG", gw), ("vV", avacc_free)), incs=(("vV", 1),))
                    else:
                        sch.add("vector", lambda e, tb=tb: e.tensor_tensor(
                            avacc[:, :], avacc[:, :], tb[:, :], ALU.add),
                            waits=(("gG", gw),), incs=(("vV", 1),))
                    avtmp_free[gg % 2] = sch.c("vV")
                av_gps_done = sch.c("gG")
                vlast = sch.c("vV")
                sch.add("vector", lambda e: e.tensor_copy(avb[:, :], avacc[:, :]),
                        waits=(("vV", vlast),), incs=(("vV", 1),))
                avacc_free = sch.c("vV")
                avbw = sch.c("vV")
                for i in range(8):
                    w = [("vV", avbw), ("vV", psT_free)] if i == 0 else []
                    sch.add("tensor", lambda e, i=i: e.transpose(
                        psT[:, 128 * i:128 * (i + 1)], avb[:, 128 * i:128 * (i + 1)],
                        identb[:, :]),
                        waits=w, incs=((("tT", 1),) if i == 7 else ()))
                tw = sch.c("tT")
                sch.add("vector", lambda e: e.tensor_copy(avT[:, :], psT[:, :]),
                        waits=(("tT", tw),), incs=(("vV", 1),))
                psT_free = sch.c("vV")
                avTw = sch.c("vV")
                for dh in range(2):
                    ps = (psA, psB)[dh]
                    for cc in range(8):
                        w = []
                        if cc == 0:
                            w = [("vV", avTw), ("vV", psab_free[dh])]
                        sch.add("tensor", mm(
                            ps[:, 0:512], avT[:, 128 * cc:128 * (cc + 1)],
                            wob[:, 1024 * cc + 512 * dh:1024 * cc + 512 * (dh + 1)],
                            start=(cc == 0), stop=(cc == 7)),
                            waits=w, incs=((("tT", 1),) if cc == 7 else ()))
                    tw = sch.c("tT")
                    wv1 = [("tT", tw)]
                    if dh == 0:
                        wv1.append(("dZ0", v1_war))
                    sch.add("vector", lambda e, dh=dh, ps=ps: e.scalar_tensor_tensor(
                        out=v1til[:, 512 * dh:512 * (dh + 1)],
                        in0=bob[:, 512 * dh:512 * (dh + 1)], scalar=1.0,
                        in1=ps[:, 0:512], op0=ALU.mult, op1=ALU.add),
                        waits=tuple(wv1))
                    sch.add("vector", lambda e, dh=dh: e.tensor_tensor(
                        v1til[:, 512 * dh:512 * (dh + 1)], v1til[:, 512 * dh:512 * (dh + 1)],
                        ut[:, 512 * dh:512 * (dh + 1)], ALU.add), incs=(("vV", 1),))
                    psab_free[dh] = sch.c("vV")
                v1w = sch.c("vV")
                dst = AP(v1_s, 128 * tt * DM, [[DM, P], [1, DM]])
                sch.add("sync", dma(dst, v1til[:, :]),
                        waits=(("vV", v1w),), incs=(("dZ0", 16),))
                v1_war = sch.c("dZ0")
                v1_st_tok = ("dZ0", sch.c("dZ0"))
                sch.add("vector", lambda e: e.scalar_tensor_tensor(
                    out=avacc[:, :], in0=v1til[:, :], scalar=1.0, in1=v1til[:, :],
                    op0=ALU.mult, op1=ALU.mult, accum_out=sumsG[:, 0:1]),
                    incs=(("vV", 1),))
                vw = sch.c("vV")
                sch.add("vector", lambda e: e.tensor_scalar(
                    sumsG[:, 1:2], sumsG[:, 0:1], 1.0 / DM, None, ALU.mult),
                    waits=(("vV", vw),), incs=(("vV", 1),))
                vw = sch.c("vV")
                sch.add("scalar", lambda e: e.activation(
                    sumsG[:, 2:3], sumsG[:, 1:2], AF.Sqrt, bias=cst[:, 2:3], scale=1.0),
                    waits=(("vV", vw),), incs=(("aA", 1),))
                aw = sch.c("aA")
                sch.add("vector", lambda e: e.reciprocal(sumsG[:, 3:4], sumsG[:, 2:3]),
                        waits=(("aA", aw),), incs=(("vV", 1),))
                vw = sch.c("vV")
                sch.add("vector", lambda e: e.tensor_scalar(
                    h2b[:, :], v1til[:, :], sumsG[:, 3:4], None, ALU.mult),
                    waits=(("vV", vw),))
                sch.add("vector", lambda e: e.tensor_tensor(
                    h2b[:, :], h2b[:, :], g2b[:, :], ALU.mult), incs=(("vV", 1),))
                h2w = sch.c("vV")
                for i in range(8):
                    w = [("vV", h2w), ("vV", psT_free)] if i == 0 else []
                    sch.add("tensor", lambda e, i=i: e.transpose(
                        psT[:, 128 * i:128 * (i + 1)], h2b[:, 128 * i:128 * (i + 1)],
                        identb[:, :]),
                        waits=w, incs=((("tT", 1),) if i == 7 else ()))
                tw = sch.c("tT")
                dstT = h2T[:, :].rearrange("p (c t) -> p c t", c=8, t=TLOC)[:, :, 128 * tt:128 * (tt + 1)]
                sch.add("vector", lambda e, d=dstT: e.tensor_copy(
                    d, psT[:, :].rearrange("p (c t) -> p c t", c=8, t=128)),
                    waits=(("tT", tw),), incs=(("vV", 1),))
                psT_free = sch.c("vV")
                h2T_ready = sch.c("vV")
                for cc in range(8):
                    w = [("vV", h2T_ready), ("vV", psD_free)] if cc == 0 else []
                    sch.add("tensor", mm(psD[:, 0:8],
                                         h2T[:, TLOC * cc + 128 * tt:TLOC * cc + 128 * (tt + 1)],
                                         wgb[:, 8 * cc:8 * (cc + 1)],
                                         start=(cc == 0), stop=(cc == 7)),
                            waits=w, incs=((("tT", 1),) if cc == 7 else ()))
                tw = sch.c("tT")
                sch.add("vector", lambda e: e.scalar_tensor_tensor(
                    out=gtile[:, :], in0=bgb[:, :], scalar=1.0, in1=psD[:, 0:8],
                    op0=ALU.mult, op1=ALU.add), waits=(("tT", tw),), incs=(("vV", 1),))
                psD_free = sch.c("vV")
                sch.add("vector", lambda e: e.reduce_max(
                    sumsG[:, 4:5], gtile[:, :], axis=AX.X),
                    waits=(("vV", psD_free),), incs=(("vV", 1),))
                vw = sch.c("vV")
                sch.add("vector", lambda e: e.tensor_scalar(
                    gtile[:, :], gtile[:, :], sumsG[:, 4:5], None, ALU.subtract),
                    waits=(("vV", vw),), incs=(("vV", 1),))
                vw = sch.c("vV")
                sch.add("scalar", lambda e: e.activation(gtile[:, :], gtile[:, :], AF.Exp),
                        waits=(("vV", vw),), incs=(("aA", 1),))
                aw = sch.c("aA")
                sch.add("vector", lambda e: e.reduce_sum(
                    sumsG[:, 5:6], gtile[:, :], axis=AX.X), waits=(("aA", aw),),
                    incs=(("vV", 1),))
                sch.add("vector", None, waits=(("vV", sch.c("vV")),))
                sch.add("vector", lambda e: e.reciprocal(sumsG[:, 6:7], sumsG[:, 5:6]),
                        incs=(("vV", 1),))
                vw = sch.c("vV")
                sch.add("vector", lambda e: e.tensor_scalar(
                    gtile[:, :], gtile[:, :], sumsG[:, 6:7], None, ALU.mult),
                    waits=(("vV", vw),), incs=(("vV", 1),))
                gw = sch.c("vV")
                sch.add("tensor", lambda e: e.transpose(
                    psC[0:8, 640:768], gtile[:, :], identf[:, :]),
                    waits=(("vV", gw), ("vV", gate_done)), incs=(("tT", 1),))
                tw = sch.c("tT")
                sch.add("vector", lambda e, tt=tt: e.tensor_copy(
                    gateT[0:8, 128 * tt:128 * (tt + 1)], psC[0:8, 640:768]),
                    waits=(("tT", tw),), incs=(("vV", 1),))
                gate_done = sch.c("vV")

            emit_pre(0)
            emit_pre(1)
            for tt in range(NT_L):
                emit_post(tt)
                if tt + 2 < NT_L:
                    emit_pre(tt + 2)
        scE13.close()

        sch.barrier()

        scE5 = Scope(nc)
        gbce_all = scE5.t("gbce_all", [P, 8192], F32)
        selrowE = scE5.t("selrowE", [1, 1024], F32)
        xeh = scE5.t("xeh", [P, 8192], BF16)
        westg0 = scE5.t("westg0", [P, 2048], F32)
        westg1 = scE5.t("westg1", [P, 2048], F32)
        webf0 = scE5.t("webf0", [P, 4096], BF16)
        webf1 = scE5.t("webf1", [P, 4096], BF16)
        combT = scE5.t("combT", [P, 8192], F32)
        cnat = scE5.t("cnat", [P, 1024], F32)
        v1r = scE5.t("v1r", [P, 1024], F32)
        x1r = scE5.t("x1r", [P, 1024], F32)
        fot = scE5.t("fot", [P, 1024], F32)
        yt_ = scE5.t("yt", [P, 1024], F32)
        sqscrF = scE5.t("sqscrF", [P, 1024], F32)
        sumsF = scE5.t("sumsF", [P, 8], F32)
        rstdF = scE5.t("rstdF", [P, 1], F32)
        nmrF = scE5.t("nmrF", [P, 1], F32)
        if True:
            westg = (westg0, westg1)
            webf = (webf0, webf1)

            acc_slices = [psA[:, 0:512], psA[:, 512:1024], psB[:, 0:512],
                          psB[:, 512:1024], psC[:, 0:512], psC[:, 512:1024],
                          psD[:, 0:512], psT[:, :].bitcast(F32)]
            we_free = [0, 0]
            webf_free = [0, 0]
            xeh_mm_done = 0
            xeh_war = 0
            accs_free = 0
            combT_free = 0
            xe_wr_done = {}
            cnat_done = 0
            fo_done = 0
            yadd_done = 0
            y_st_done = 0
            for half in range(2):
                tof = half * 1024
                gb_start = sch.c("vV")
                for e_i in range(NEXP):
                    for hcol in range(2):
                        w = []
                        if e_i == 0 and hcol == 0:
                            w = [("aA", accs_free), ("vV", cnat_done)]
                        sch.add("tensor", mm(psA[0:1, 512 * hcol:512 * (hcol + 1)],
                                             identf[0:32, e_i:e_i + 1],
                                             gateT[0:32, tof + 512 * hcol:tof + 512 * (hcol + 1)],
                                             start=True, stop=True),
                                waits=w, incs=((("tT", 1),) if hcol == 1 else ()))
                    tg = sch.c("tT")
                    sch.add("vector", lambda e: e.tensor_copy(selrowE[0:1, :], psA[0:1, 0:1024]),
                            waits=(("tT", tg),), incs=(("vV", 1),))
                    sg = sch.c("vV")
                    for hcol in range(2):
                        sch.add("tensor", lambda e, hcol=hcol: e.matmul(
                            psB[:, 512 * hcol:512 * (hcol + 1)], ones1[:, :],
                            selrowE[0:1, 512 * hcol:512 * (hcol + 1)],
                            start=True, stop=True),
                            waits=(("vV", sg),), incs=((("tT", 1),) if hcol == 1 else ()))
                    tg = sch.c("tT")
                    sch.add("vector", lambda e, e_i=e_i: e.tensor_copy(
                        gbce_all[:, 1024 * e_i:1024 * (e_i + 1)], psB[:, :]),
                        waits=(("tT", tg),), incs=(("vV", 1),))
                gb_ready = sch.c("vV")
                for dh in range(2):
                    for e_i in range(NEXP):
                        wbuf = e_i % 2
                        for cc in range(8):
                            w = []
                            if cc == 0:
                                w = [("vV", gb_ready), ("tT", xeh_mm_done)]
                            sch.add("vector", lambda e, cc=cc, tof=tof, e_i=e_i: e.tensor_tensor(
                                xeh[:, 1024 * cc:1024 * (cc + 1)],
                                h2T[:, TLOC * cc + tof:TLOC * cc + tof + 1024],
                                gbce_all[:, 1024 * e_i:1024 * (e_i + 1)], ALU.mult),
                                waits=w,
                                incs=((("vV", 1),) if cc == 7 else ()))
                        xel = ("vV", sch.c("vV"))
                        for pi, piece in enumerate(range(0, 8, 4)):
                            stg = westg[pi]
                            src = AP(f_we, e_i * (2 * DM * 512) + dh * (DM * 512)
                                     + piece * P * 512,
                                     [[512, P], [P * 512, 4], [1, 512]])
                            sch.add("sync", dma(stg[:, :], src),
                                    waits=(("vV", we_free[pi]),), incs=((f"dW{pi}", 16),))
                            dl = sch.c(f"dW{pi}")
                            sch.add("vector", lambda e, s=stg, b=webf[wbuf],
                                    o=piece * 512, pi=pi: e.tensor_copy(b[:, o:o + 2048], s[:, :]),
                                    waits=((f"dW{pi}", dl), ("tT", webf_free[wbuf])),
                                    incs=(("vV", 1),))
                            we_free[pi] = sch.c("vV")
                        wew = sch.c("vV")
                        for cc in range(8):
                            for d4 in range(4):
                                for p2 in range(2):
                                    acc = acc_slices[d4 * 2 + p2]
                                    w = []
                                    if cc == 0 and d4 == 0 and p2 == 0:
                                        w = [xel, ("vV", wew)]
                                        if e_i == 0:
                                            w.append(("aA", accs_free))
                                            w.append(("vV", gb_ready))
                                    sch.add("tensor", mm(
                                        acc,
                                        webf[wbuf][:, 512 * cc + 128 * d4:512 * cc + 128 * (d4 + 1)],
                                        xeh[:, 1024 * cc + 512 * p2:1024 * cc + 512 * (p2 + 1)],
                                        start=(e_i == 0 and cc == 0),
                                        stop=(e_i == NEXP - 1 and cc == 7)),
                                        waits=w,
                                        incs=((("tT", 1),) if (cc == 7 and d4 == 3 and p2 == 1) else ()))
                        xeh_mm_done = sch.c("tT")
                        webf_free[wbuf] = sch.c("tT")
                    tw = sch.c("tT")
                    for d4 in range(4):
                        for p2 in range(2):
                            acc = acc_slices[d4 * 2 + p2]
                            o = (4 * dh + d4) * 1024 + 512 * p2
                            w = []
                            if d4 == 0 and p2 == 0:
                                w = [("tT", tw), ("tT", combT_free)]
                            sch.add("scalar", lambda e, a=acc, o=o: e.copy(
                                combT[:, o:o + 512], a),
                                waits=w,
                                incs=((("aA", 1),) if (d4 == 3 and p2 == 1) else ()))
                    accs_free = sch.c("aA")
                if debug:
                    sch.add("sync", dma(dbg_comb.ap()[half], combT[:, :]),
                            waits=(("aA", accs_free),), incs=(("dDbg", 16),))
                dbg_comb_tok = sch.c("dDbg")
                for tl in range(8):
                    tt = half * 8 + tl
                    for i in range(8):
                        w = []
                        if i == 0:
                            w = [("aA", accs_free), ("vV", cnat_done)]
                            if debug:
                                w.append(("dDbg", dbg_comb_tok))
                        sch.add("tensor", lambda e, i=i, tl=tl: e.transpose(
                            psC[:, 128 * i:128 * (i + 1)],
                            combT[:, 1024 * i + 128 * tl:1024 * i + 128 * (tl + 1)],
                            identf[:, :]),
                            waits=w, incs=((("tT", 1),) if i == 7 else ()))
                    tw = sch.c("tT")
                    if tl == 7:
                        combT_free = tw
                    sch.add("vector", lambda e: e.tensor_copy(cnat[:, :], psC[:, 0:1024]),
                            waits=(("tT", tw),), incs=(("vV", 1),))
                    cnat_done = sch.c("vV")
                    src = AP(v1_s, 128 * tt * DM, [[DM, P], [1, DM]])
                    sch.add("gpsimd", dma(v1r[:, :], src),
                            waits=(("vV", yadd_done), v1_st_tok), incs=(("dG", 16),))
                    src = AP(x1_s, 128 * tt * DM, [[DM, P], [1, DM]])
                    sch.add("gpsimd", dma(x1r[:, :], src), incs=(("dG", 16),))
                    dl = sch.c("dG")
                    sch.add("vector", lambda e: e.tensor_tensor(
                        fot[:, :], cnat[:, :], v1r[:, :], ALU.add),
                        waits=(("dG", dl),), incs=(("vV", 1),))
                    fo_done = sch.c("vV")
                    vw = emit_stats(fot[:, :], DM, EPS_LN, rstdF[:, 0:1], nmrF[:, 0:1],
                                    sumsF, sqscrF)
                    sch.add("vector", lambda e: e.tensor_scalar(
                        fot[:, :], fot[:, :], rstdF[:, 0:1], nmrF[:, 0:1],
                        ALU.mult, ALU.add), waits=(("vV", vw),))
                    sch.add("vector", lambda e: e.tensor_tensor(
                        yt_[:, :], fot[:, :], x1r[:, :], ALU.add),
                        waits=(("dZ0", y_st_done),), incs=(("vV", 1),))
                    yadd_done = sch.c("vV")
                    vw = sch.c("vV")
                    dst = AP(y_out, 128 * tt * DM, [[DM, P], [1, DM]])
                    sch.add("sync", dma(dst, yt_[:, :]), waits=(("vV", vw),),
                            incs=(("dZ0", 16),))
                    y_st_done = sch.c("dZ0")

        scE5.close()

    allds = [(nm, sch.cnt.get(nm, 0)) for nm in
             ("dC0", "dC1", "dW0", "dW1", "dX0", "dX1", "dY0", "dY1",
              "dF", "dR", "dH0", "dH1", "dZ0", "dE", "dG", "dB", "dDbg")]
    sch.add("sync", None, waits=tuple(allds), incs=(("ph", 1),))

    nc._sch_ref = sch
    semnames = sorted({s for e in ENGS for (w, _, ii) in sch.items[e]
                       for (s, _v) in list(w) + list(ii)})
    from contextlib import ExitStack
    with ExitStack() as stk:
        sems = {s: stk.enter_context(nc.semaphore(s)) for s in semnames}
        with nc.Block() as block:
            def runner(name):
                def run(eng):
                    for (w, fn, ii) in sch.items[name]:
                        for (s, v) in w:
                            eng.wait_ge(sems[s], v)
                        if fn is None:
                            for (s, n) in ii:
                                eng.sem_inc(sems[s], n)
                        else:
                            inst = fn(eng)
                            for (s, n) in ii:
                                inst = inst.then_inc(sems[s], n)
                return run
            block.sync(runner("sync"))
            block.tensor(runner("tensor"))
            block.vector(runner("vector"))
            block.scalar(runner("scalar"))
            block.gpsimd(runner("gpsimd"))
    return nc


def make_in_maps(inputs):
    x = np.ascontiguousarray(inputs["L0_attn_x"], dtype=np.float32)
    rel = np.asarray(inputs["L0_attn_rel_table"], dtype=np.float32)

    relT_full = np.empty((16, 2600), np.float32)
    relT_full[:, :2047] = rel.T
    relT_full[:, 2047:] = rel.T[:, 2046:2047]
    winf128 = np.ascontiguousarray(rel[2046, np.arange(P) % 16].reshape(P, 1))

    def bcast(v, n):
        return np.ascontiguousarray(np.broadcast_to(np.asarray(v, np.float32)[None, :], (P, n)))

    shared = {
        "a_wq": np.ascontiguousarray(inputs["L0_attn_Wq"], np.float32),
        "a_wk": np.ascontiguousarray(inputs["L0_attn_Wk"], np.float32),
        "a_wv": np.ascontiguousarray(
            np.stack([np.asarray(inputs["L0_attn_Wv"], np.float32)[:, :512],
                      np.asarray(inputs["L0_attn_Wv"], np.float32)[:, 512:]])),
        "a_gamma": np.ascontiguousarray(inputs["L0_attn_gamma"], np.float32),
        "a_beta": np.ascontiguousarray(inputs["L0_attn_beta"], np.float32),
        "winf128": winf128,
        "gammab_d": bcast(inputs["L0_attn_gamma"], 1024),
        "betab_d": bcast(inputs["L0_attn_beta"], 1024),
        "g1b_d": bcast(inputs["L0_ffn_gamma1"], 1024),
        "g2b_d": bcast(inputs["L0_ffn_gamma2"], 1024),
        "bqb_d": bcast(inputs["L0_ffn_bq"], 1024),
        "bdnb_d": bcast(inputs["L0_ffn_bdown"], 64),
        "bupb_d": bcast(inputs["L0_ffn_bup"], 2048),
        "bob_d": bcast(inputs["L0_ffn_bo"], 1024),
        "bgb_d": bcast(inputs["L0_ffn_bg"], 8),
        "f_g1": np.ascontiguousarray(inputs["L0_ffn_gamma1"], np.float32),
        "f_wq": np.ascontiguousarray(inputs["L0_ffn_Wq"], np.float32),
        "f_bq": np.ascontiguousarray(inputs["L0_ffn_bq"], np.float32),
        "f_wdn": np.ascontiguousarray(inputs["L0_ffn_Wdown"], np.float32),
        "f_bdn": np.ascontiguousarray(inputs["L0_ffn_bdown"], np.float32),
        "f_wup": np.ascontiguousarray(
            np.vstack([np.asarray(inputs["L0_ffn_Wup"], np.float32),
                       np.zeros((64, 2048), np.float32)])),
        "f_bup": np.ascontiguousarray(inputs["L0_ffn_bup"], np.float32),
        "f_wo": np.ascontiguousarray(inputs["L0_ffn_Wo"], np.float32),
        "f_bo": np.ascontiguousarray(inputs["L0_ffn_bo"], np.float32),
        "f_g2": np.ascontiguousarray(inputs["L0_ffn_gamma2"], np.float32),
        "f_wg": np.ascontiguousarray(inputs["L0_ffn_Wg"], np.float32),
        "f_bg": np.ascontiguousarray(inputs["L0_ffn_bg"], np.float32),
        "f_we": np.ascontiguousarray(
            np.stack([np.stack([w[:, :512], w[:, 512:]])
                      for w in np.asarray(inputs["L0_ffn_We"], np.float32)])),
        "identf_d": np.eye(P, dtype=np.float32),
        "identb_d": np.eye(P).astype(ml_dtypes.bfloat16),
    }

    in_maps = []
    for cid in range(8):
        b, g = cid // 4, cid % 4
        rw = np.empty((len(DPAIRS), P, WINW), np.float32)
        parr = np.arange(P)
        for pidx, (h, j) in enumerate(DPAIRS):
            K = S - 1 - 512 * h + 512 * j
            c_base = K - 128 * g - 127
            for c in range(WINW):
                rw[pidx, :, c] = relT_full[parr % 16, c_base + c + parr // 16]
        m = dict(shared)
        m["x_b"] = x[b]
        m["x_loc"] = np.ascontiguousarray(x[b, TLOC * g:TLOC * (g + 1)])
        m["relwin"] = rw
        in_maps.append(m)

    return in_maps


def kernel(**inputs):
    in_maps = make_in_maps(inputs)
    nc = build_program()
    res = run_bass_kernel_spmd(nc, in_maps, core_ids=list(range(8)))
    y = np.empty((2, S, DM), np.float32)
    for cid in range(8):
        b, g = cid // 4, cid % 4
        y[b, TLOC * g:TLOC * (g + 1)] = res.results[cid]["y"]
    return y


# version 13
# speedup vs baseline: 1.1878x; 1.1878x over previous
import numpy as np
import ml_dtypes
import concourse.bass as bass
import concourse.mybir as mybir
from concourse.ap import AP
from concourse.bass_utils import run_bass_kernel_spmd

F32 = mybir.dt.float32
BF16 = mybir.dt.bfloat16
AF = mybir.ActivationFunctionType
ALU = mybir.AluOpType
AX = mybir.AxisListType

P = 128
DM = 1024
S = 8192
H = 16
HD = 64
TLOC = 2048
NPAN_F = 16
NPAN_L = 4
NT_L = 16
FCH = 64
EPS_LN = 1e-5
EPS_ATT = 1e-3
EPS_RMS = 1e-6
L_FF = 64
NEXP = 8
DPAIRS = [(12, 0), (13, 0), (13, 1), (14, 0), (14, 1), (14, 2),
          (15, 0), (15, 1), (15, 2), (15, 3)]
WINW = 640
WROW = 656

ENGS = ("sync", "tensor", "vector", "scalar", "gpsimd")


class Sch:
    def __init__(self):
        self.items = {e: [] for e in ENGS}
        self.cnt = {}
        self.pend = {e: [] for e in ENGS}
        self.nbar = 0
        self.ds = "dS0"

    def set_ds(self, name):
        self.ds = name

    def _m(self, s):
        return self.ds if s == "dS" else s

    def c(self, s):
        return self.cnt.get(self._m(s), 0)

    def tok(self, s):
        s = self._m(s)
        return (s, self.cnt.get(s, 0))

    def add(self, eng, fn, waits=(), incs=()):
        w = [(self._m(s), v) for (s, v) in list(self.pend[eng]) + list(waits) if v > 0]
        self.pend[eng] = []
        ii = []
        for (s, n) in incs:
            s = self._m(s)
            self.cnt[s] = self.cnt.get(s, 0) + n
            ii.append((s, n))
        self.items[eng].append((w, fn, ii))

    def barrier(self):
        self.nbar += 1
        tgt = 5 * self.nbar
        for e in ENGS:
            self.add(e, None, incs=(("ph", 1),))
        for e in ENGS:
            self.pend[e].append(("ph", tgt))




class Scope:

    def __init__(self, nc):
        from contextlib import ExitStack
        self.nc = nc
        self.stk = ExitStack()

    def t(self, name, shape, dt):
        return self.stk.enter_context(self.nc.sbuf_tensor(name, shape, dt))

    def close(self):
        self.stk.close()


def build_program(debug=False):
    nc = bass.Bass("TRN2", target_bir_lowering=False)

    def din(name, shape, dt=F32):
        return nc.dram_tensor(name, list(shape), dt, kind="ExternalInput")

    x_b = din("x_b", (S, DM))
    x_loc = din("x_loc", (TLOC, DM))
    a_wq = din("a_wq", (DM, DM))
    a_wk = din("a_wk", (DM, DM))
    a_wv = din("a_wv", (2, DM, 512))
    a_gamma = din("a_gamma", (DM,))
    a_beta = din("a_beta", (DM,))
    relwin = din("relwin", (len(DPAIRS), P, WINW))
    winf128 = din("winf128", (P, 1))
    f_g1 = din("f_g1", (DM,))
    f_wq = din("f_wq", (DM, DM))
    f_bq = din("f_bq", (DM,))
    f_wdn = din("f_wdn", (DM, L_FF))
    f_bdn = din("f_bdn", (L_FF,))
    f_wup = din("f_wup", (P, 2 * DM))
    f_bup = din("f_bup", (2 * DM,))
    f_wo = din("f_wo", (DM, DM))
    f_bo = din("f_bo", (DM,))
    f_g2 = din("f_g2", (DM,))
    f_wg = din("f_wg", (DM, NEXP))
    f_bg = din("f_bg", (NEXP,))
    f_we = din("f_we", (NEXP, 2, DM, 512))
    gammab_d = din("gammab_d", (P, DM))
    betab_d = din("betab_d", (P, DM))
    g1b_d = din("g1b_d", (P, DM))
    g2b_d = din("g2b_d", (P, DM))
    bqb_d = din("bqb_d", (P, DM))
    bdnb_d = din("bdnb_d", (P, L_FF))
    bupb_d = din("bupb_d", (P, 2 * DM))
    bob_d = din("bob_d", (P, DM))
    bgb_d = din("bgb_d", (P, NEXP))
    identf_d = din("identf_d", (P, P))
    identb_d = din("identb_d", (P, P), BF16)

    y_out = nc.dram_tensor("y", [TLOC, DM], F32, kind="ExternalOutput")

    dkind = "ExternalOutput" if debug else "Internal"
    debug_v = (debug == "v")
    debug = (debug is True)
    if debug_v:
        dbg_wres = nc.dram_tensor("dbg_wres", [P, 8192], BF16, kind="ExternalOutput")
        dbg_vst = nc.dram_tensor("dbg_vst", [2, P, 8 * 4160], BF16, kind="ExternalOutput")
    xT_s = nc.dram_tensor("xT_s", [20, P, 4096], BF16, kind=dkind)
    k_s = nc.dram_tensor("k_s", [H, NPAN_F, 64, 512], BF16, kind=dkind)
    q_s = nc.dram_tensor("q_s", [H, NPAN_L, 64, 512], BF16, kind=dkind)
    v_s = nc.dram_tensor("v_s", [H, P, FCH * 65], BF16, kind=dkind)
    x1_s = nc.dram_tensor("x1_s", [TLOC, DM], F32, kind=dkind)
    v1_s = nc.dram_tensor("v1_s", [TLOC, DM], F32, kind=dkind)
    xe_s = nc.dram_tensor("xe_s", [NEXP, DM, DM], BF16, kind=dkind)
    if debug:
        dbg_outT = nc.dram_tensor("dbg_outT", [P, 8 * TLOC], F32, kind="ExternalOutput")
        dbg_den = nc.dram_tensor("dbg_den", [P, 256], F32, kind="ExternalOutput")
        dbg_delta = nc.dram_tensor("dbg_delta", [P, 2 * TLOC], F32, kind="ExternalOutput")
        dbg_cinf = nc.dram_tensor("dbg_cinf", [P, 16], F32, kind="ExternalOutput")
        dbg_st = nc.dram_tensor("dbg_st", [P, 64], F32, kind="ExternalOutput")
        dbg_comb = nc.dram_tensor("dbg_comb", [2, P, 8192], F32, kind="ExternalOutput")

    psA = nc.alloc_psum_tensor("psA", [P, 1024], F32)
    psB = nc.alloc_psum_tensor("psB", [P, 1024], F32)
    psC = nc.alloc_psum_tensor("psC", [P, 1024], F32)
    psD = nc.alloc_psum_tensor("psD", [P, 512], F32)
    psT = nc.alloc_psum_tensor("psT", [P, 1024], BF16)

    sch = Sch()

    identb = nc.alloc_sbuf_tensor("identb", [P, P], BF16)
    identf = nc.alloc_sbuf_tensor("identf", [P, P], F32)
    cst = nc.alloc_sbuf_tensor("cst", [P, 4], F32)
    ones1 = nc.alloc_sbuf_tensor("ones1", [1, P], F32)
    dbgst = nc.alloc_sbuf_tensor("dbgst", [P, 64], F32) if debug else None
    scope_AD = Scope(nc)
    den_nat = scope_AD.t("den_nat", [P, 256], F32)
    delta = scope_AD.t("delta", [P, 2 * TLOC], F32)
    cinf64 = scope_AD.t("cinf64", [P, 16], F32)

    def mm(out, lhsT, rhs, start, stop, tile_position=None):
        def f(eng, out=out, lhsT=lhsT, rhs=rhs, start=start, stop=stop, tp=tile_position):
            return eng.matmul(out, lhsT, rhs, start=start, stop=stop, tile_position=tp)
        return f

    def dma(out, in_):
        def f(eng, out=out, in_=in_):
            return eng.dma_start(out=out, in_=in_)
        return f

    def emit_stats(xap, n, eps, rstd_col, nmr_col, sums, sqscr, first_waits=()):
        eps = {EPS_LN: cst[:, 0:1], EPS_ATT: cst[:, 1:2], EPS_RMS: cst[:, 2:3]}[eps]
        inv_n = 1.0 / n
        sch.add("vector", lambda e, a=xap, o=sums: e.reduce_sum(o[:, 0:1], a, axis=AX.X),
                waits=first_waits)
        sch.add("vector", lambda e, a=xap, s=sqscr, o=sums: e.scalar_tensor_tensor(
            out=s[:, :], in0=a, scalar=1.0, in1=a, op0=ALU.mult, op1=ALU.mult,
            accum_out=o[:, 1:2]), incs=(("vV", 1),))
        w = sch.c("vV")
        sch.add("vector", lambda e, o=sums, k=inv_n: e.tensor_scalar(
            o[:, 2:3], o[:, 0:1], k, None, ALU.mult), waits=(("vV", w),))
        sch.add("vector", lambda e, o=sums, k=inv_n: e.tensor_scalar(
            o[:, 3:4], o[:, 1:2], k, None, ALU.mult), incs=(("vV", 1),))
        w = sch.c("vV")
        sch.add("vector", lambda e, o=sums: e.tensor_tensor(
            o[:, 4:5], o[:, 2:3], o[:, 2:3], ALU.mult),
            waits=(("vV", w),), incs=(("vV", 1),))
        w = sch.c("vV")
        sch.add("vector", lambda e, o=sums: e.tensor_tensor(
            o[:, 5:6], o[:, 3:4], o[:, 4:5], ALU.subtract),
            waits=(("vV", w),), incs=(("vV", 1),))
        v1 = sch.c("vV")
        sch.add("scalar", lambda e, o=sums, ep=eps: e.activation(
            o[:, 6:7], o[:, 5:6], AF.Sqrt, bias=ep, scale=1.0),
            waits=(("vV", v1),), incs=(("aA", 1),))
        a1 = sch.c("aA")
        sch.add("vector", lambda e, o=sums, r=rstd_col: e.reciprocal(r, o[:, 6:7]),
                waits=(("aA", a1),), incs=(("vV", 1),))
        w = sch.c("vV")
        sch.add("vector", lambda e, o=sums, r=rstd_col, m=nmr_col: e.scalar_tensor_tensor(
            out=m, in0=o[:, 2:3], scalar=-1.0, in1=r, op0=ALU.mult, op1=ALU.mult),
            waits=(("vV", w),), incs=(("vV", 1),))
        return sch.c("vV")

    with (
        nc.sbuf_tensor("xpan0", [P, 4096], F32) as xpan0,
        nc.sbuf_tensor("xpan1", [P, 4096], F32) as xpan1,
        nc.sbuf_tensor("sqscrA", [P, 1024], F32) as sqscrA,
        nc.sbuf_tensor("sumsA", [P, 8], F32) as sumsA,
        nc.sbuf_tensor("rstdA", [P, 1], F32) as rstdA,
        nc.sbuf_tensor("nmrA", [P, 1], F32) as nmrA,
        nc.sbuf_tensor("xtile0", [P, 1024], BF16) as xtile0,
        nc.sbuf_tensor("xtile1", [P, 1024], BF16) as xtile1,
        nc.sbuf_tensor("xtpA0", [P, 4096], BF16) as xtpA0,
        nc.sbuf_tensor("xtpA1", [P, 4096], BF16) as xtpA1,
    ):
        xpan = (xpan0, xpan1)
        xtile = (xtile0, xtile1)
        xtpA = (xtpA0, xtpA1)
        sch.add("sync", dma(identb[:, :], identb_d.ap()), incs=(("dC0", 16),))
        sch.add("sync", dma(identf[:, :], identf_d.ap()), incs=(("dC0", 16),))
        ident_ld = sch.c("dC0")
        sch.add("gpsimd", lambda e: e.memset(delta[:, :], 0.0), incs=(("gG", 1),))
        sch.add("vector", lambda e: e.memset(cst[:, 0:1], EPS_LN))
        sch.add("vector", lambda e: e.memset(cst[:, 1:2], EPS_ATT))
        sch.add("vector", lambda e: e.memset(cst[:, 2:3], EPS_RMS))
        sch.add("vector", lambda e: e.memset(ones1[:, :], 1.0), incs=(("vV", 1),))

        xtp_free_at = [0, 0]
        pan_apply_done = [0, 0]
        tile_tr_done = [0, 0]
        psT_evac_done = 0
        for pn in range(20):
            buf = pn % 2
            dX = f"dX{buf}"
            if pn < 16:
                src = AP(x_b, 512 * pn * DM, [[DM, P], [P * DM, 4], [1, DM]])
            else:
                src = AP(x_loc, 512 * (pn - 16) * DM, [[DM, P], [P * DM, 4], [1, DM]])
            sch.add("sync", dma(xpan[buf][:, :], src),
                    waits=(("vV", pan_apply_done[buf]),), incs=((dX, 16),))
            pld = (dX, sch.c(dX))
            for tt in range(4):
                xsl = xpan[buf][:, 1024 * tt:1024 * (tt + 1)]
                vw = emit_stats(xsl, DM, EPS_LN, rstdA[:, 0:1], nmrA[:, 0:1],
                                sumsA, sqscrA, first_waits=(pld,))
                tbuf = xtile[tt % 2]
                sch.add("vector", lambda e, a=xsl, t=tbuf: e.tensor_scalar(
                    t[:, :], a, rstdA[:, 0:1], nmrA[:, 0:1], ALU.mult, ALU.add),
                    waits=(("vV", vw), ("tT", tile_tr_done[tt % 2])),
                    incs=(("vV", 1),))
                vap = sch.c("vV")
                if debug and pn == 0:
                    sch.add("vector", lambda e, tt=tt: e.tensor_copy(
                        dbgst[:, 12 * tt:12 * tt + 8], sumsA[:, 0:8]),
                        waits=(("vV", vap),))
                    sch.add("vector", lambda e, tt=tt: e.tensor_copy(
                        dbgst[:, 12 * tt + 8:12 * tt + 9], rstdA[:, 0:1]))
                    sch.add("vector", lambda e, tt=tt: e.tensor_copy(
                        dbgst[:, 12 * tt + 9:12 * tt + 10], nmrA[:, 0:1]))
                for i in range(8):
                    w = []
                    if i == 0:
                        w = [("vV", vap), ("dC0", ident_ld), ("vV", psT_evac_done)]
                    sch.add("tensor", lambda e, t=tbuf, i=i: e.transpose(
                        psT[:, 128 * i:128 * (i + 1)], t[:, 128 * i:128 * (i + 1)],
                        identb[:, :]),
                        waits=w, incs=((("tT", 1),) if i == 7 else ()))
                tile_tr_done[tt % 2] = sch.c("tT")
                dst = xtpA[buf][:, :].rearrange("p (c t) -> p c t", c=8, t=512)[:, :, 128 * tt:128 * (tt + 1)]
                w = [("tT", sch.c("tT"))]
                if tt == 0:
                    w.append((f"dY{buf}", xtp_free_at[buf]))
                sch.add("vector", lambda e, d=dst: e.tensor_copy(
                    d, psT[:, :].rearrange("p (c t) -> p c t", c=8, t=128)),
                    waits=w, incs=(("vV", 1),))
                psT_evac_done = sch.c("vV")
            pan_apply_done[buf] = sch.c("vV")
            sch.add("sync", dma(xT_s.ap()[pn], xtpA[buf][:, :]),
                    waits=(("vV", sch.c("vV")),), incs=((f"dY{buf}", 16),))
            xtp_free_at[buf] = sch.c(f"dY{buf}")
        a1_store_toks = (("dY0", sch.c("dY0")), ("dY1", sch.c("dY1")))

    sch.barrier()

    with (
        nc.sbuf_tensor("wstage0", [P, 2048], F32) as wstage0,
        nc.sbuf_tensor("wstage1", [P, 2048], F32) as wstage1,
        nc.sbuf_tensor("wres", [P, 8192], BF16) as wres,
        nc.sbuf_tensor("xtpin0", [P, 4096], BF16) as xtpin0,
        nc.sbuf_tensor("xtpin1", [P, 4096], BF16) as xtpin1,
        nc.sbuf_tensor("pevac0", [P, 4096], BF16) as pevac0,
        nc.sbuf_tensor("pevac1", [P, 4096], BF16) as pevac1,
        nc.sbuf_tensor("vstage", [P, 8 * 4160], BF16) as vstage,
        nc.sbuf_tensor("winfb", [P, 1], BF16) as winfb,
        nc.sbuf_tensor("winff", [P, 1], F32) as winff,
        nc.sbuf_tensor("widef", [P, WINW], F32) as widef,
        nc.sbuf_tensor("wideb", [P, WINW], BF16) as wideb,
        nc.sbuf_tensor("crev", [P, 128], F32) as crev,
    ):
        wstage = (wstage0, wstage1)
        xtpin = (xtpin0, xtpin1)
        pevac = (pevac0, pevac1)
        wst_free = [0, 0]
        xtpin_free = [0, 0]
        pevac_free = [("dY0", 0), ("dY1", 0)]
        psab_free = [0, 0]

        def load_weight_cast(wdram, wbf, ncols, scale):
            for piece in range(0, 8, 2):
                bi = (piece // 2) % 2
                stg = wstage[bi]
                src = AP(wdram, piece * P * ncols,
                         [[ncols, P], [P * ncols, 2], [1, ncols]])
                sch.add("sync", dma(stg[:, 0:2 * ncols], src),
                        waits=(("vV", wst_free[bi]),), incs=((f"dW{bi}", 16),))
                dl = sch.c(f"dW{bi}")
                sch.add("vector", lambda e, s=stg, n=2 * ncols, d=wbf,
                        o=piece * ncols, sc=scale: e.tensor_scalar(
                            d[:, o:o + n], s[:, 0:n], sc, None, ALU.mult),
                        waits=((f"dW{bi}", dl),), incs=(("vV", 1),))
                wst_free[bi] = sch.c("vV")
            return sch.c("vV")

        def proj_sweep(panels, out_dram, npan, wready):
            for idx, pn in enumerate(panels):
                buf = idx % 2
                sch.add("sync", dma(xtpin[buf][:, :], xT_s.ap()[pn]),
                        waits=(("tT", xtpin_free[buf]),) + a1_store_toks,
                        incs=((f"dX{buf}", 16),))
                dl = sch.c(f"dX{buf}")
                for dcc in range(8):
                    ps = (psA, psB)[dcc % 2]
                    for ccc in range(8):
                        w = []
                        if ccc == 0:
                            w = [(f"dX{buf}", dl), ("vV", wready), ("aA", psab_free[dcc % 2])]
                        sch.add("tensor", mm(
                            ps[:, 0:512],
                            wres[:, ccc * 1024 + dcc * P: ccc * 1024 + (dcc + 1) * P],
                            xtpin[buf][:, 512 * ccc:512 * (ccc + 1)],
                            start=(ccc == 0), stop=(ccc == 7)),
                            waits=w, incs=((("tT", 1),) if ccc == 7 else ()))
                    tw = sch.c("tT")
                    w = [("tT", tw)]
                    if dcc == 0:
                        w.append(pevac_free[buf])
                    sch.add("scalar", lambda e, d=pevac[buf], o=dcc * 512, ps=ps: e.copy(
                        d[:, o:o + 512], ps[:, 0:512]),
                        waits=w, incs=(("aA", 1),))
                    psab_free[dcc % 2] = sch.c("aA")
                xtpin_free[buf] = sch.c("tT")
                blk = npan * 64 * 512
                for hh in range(2):
                    dst = AP(out_dram, idx * 64 * 512 + hh * blk,
                             [[512, 64], [2 * blk, 8], [1, 512]])
                    srcv = pevac[buf][64 * hh:64 * hh + 64, :].rearrange(
                        "p (c t) -> p c t", c=8, t=512)
                    sch.add("sync", dma(dst, srcv),
                            waits=(("aA", sch.c("aA")),) if hh == 0 else (),
                            incs=((f"dY{buf}", 16),))
                pevac_free[buf] = (f"dY{buf}", sch.c(f"dY{buf}"))

        wr = load_weight_cast(a_wk, wres, 1024, 1.0)
        proj_sweep(list(range(16)), k_s, NPAN_F, wr)
        sch.barrier()

        wr = load_weight_cast(a_wq, wres, 1024, 1.0 / 32.0)
        proj_sweep([16, 17, 18, 19], q_s, NPAN_L, wr)
        sch.barrier()

        sch.add("sync", dma(winff[:, :], winf128.ap()), incs=(("dC1", 16),))
        dwf = sch.c("dC1")
        sch.add("vector", lambda e: e.tensor_copy(winfb[:, :], winff[:, :]),
                waits=(("dC1", dwf),), incs=(("vV", 1),))
        winf_ready = sch.c("vV")

        vevac_done = [0, 0]
        cinf_evac = 0
        wide_mm_done = 0
        wide_cast_done = 0
        delta_add_done = 0
        vflush_prev = 0
        for gv in range(2):
            wres_free = sch.c("tT")
            for piece in range(0, 8, 2):
                bi = (piece // 2) % 2
                stg = wstage[bi]
                src = AP(a_wv, gv * DM * 512 + piece * P * 512,
                         [[512, P], [P * 512, 2], [1, 512]])
                sch.add("sync", dma(stg[:, 0:1024], src),
                        waits=(("vV", wst_free[bi]),), incs=((f"dW{bi}", 16),))
                dl = sch.c(f"dW{bi}")
                sch.add("vector", lambda e, s=stg, d=wres, o=piece * 512: e.tensor_scalar(
                    d[:, o:o + 1024], s[:, 0:1024], 1.0, None, ALU.mult),
                    waits=((f"dW{bi}", dl), ("tT", wres_free)), incs=(("vV", 1),))
                wst_free[bi] = sch.c("vV")
            wv_ready = sch.c("vV")
            sch.add("vector", lambda e: e.memset(
                vstage[:, :].rearrange("p (h c o) -> p h c o", h=8, c=64, o=65)[:, :, :, 64:65],
                1.0), waits=(("dF", vflush_prev), ("tT", wres_free)), incs=(("vV", 1),))
            ones_ready = sch.c("vV")

            for pn in range(16):
                buf = pn % 2
                sch.add("sync", dma(xtpin[buf][:, :], xT_s.ap()[pn]),
                        waits=(("tT", xtpin_free[buf]),), incs=((f"dX{buf}", 16),))
                dl = sch.c(f"dX{buf}")
                for tt in range(4):
                    ps = (psA, psB)[tt % 2]
                    for ccc in range(8):
                        w = []
                        if ccc == 0:
                            w = [(f"dX{buf}", dl), ("vV", wv_ready), ("vV", ones_ready),
                                 ("aA", vevac_done[tt % 2])]
                        sch.add("tensor", mm(
                            ps[:, 0:512],
                            xtpin[buf][:, 512 * ccc + 128 * tt: 512 * ccc + 128 * (tt + 1)],
                            wres[:, 512 * ccc:512 * (ccc + 1)],
                            start=(ccc == 0), stop=(ccc == 7)),
                            waits=w, incs=((("tT", 1),) if ccc == 7 else ()))
                    tw = sch.c("tT")
                    ct = pn * 4 + tt
                    dst = vstage[:, :].rearrange("p (h c o) -> p h c o",
                                                 h=8, c=64, o=65)[:, :, ct, 0:64]
                    sch.add("scalar", lambda e, d=dst, ps=ps: e.copy(
                        d, ps[:, 0:512].rearrange("p (h d) -> p h d", h=8, d=64)),
                        waits=(("tT", tw),), incs=(("aA", 1),))
                    vevac_done[tt % 2] = sch.c("aA")
                xtpin_free[buf] = sch.c("tT")
            vdone = sch.c("aA")

            if debug_v:
                sch.add("sync", dma(dbg_vst.ap()[gv], vstage[:, :]),
                        waits=(("aA", vdone),), incs=(("dF", 16),))
            for hg in range(8):
                sch.add("sync", dma(v_s.ap()[8 * gv + hg],
                                    vstage[:, 4160 * hg:4160 * (hg + 1)]),
                        waits=(("aA", vdone),), incs=(("dF", 16),))
            vflush_prev = sch.c("dF")

            for hg in range(8):
                h = 8 * gv + hg
                base = 64 * (h % 2)
                tp = (0, 64) if (h % 2) else (0, 0)
                for cb in range(FCH):
                    w = []
                    if cb == 0:
                        w = [("aA", vdone), ("vV", winf_ready), ("vV", cinf_evac)]
                    sch.add("tensor", mm(psC[base:base + 64, 0:1],
                                         vstage[:, 4160 * hg + 65 * cb:4160 * hg + 65 * cb + 64],
                                         winfb[:, :], start=(cb == 0), stop=(cb == FCH - 1),
                                         tile_position=tp),
                            waits=w, incs=((("tT", 1),) if cb == FCH - 1 else ()))
                tw = sch.c("tT")
                sch.add("vector", lambda e, b=base, h=h: e.tensor_copy(
                    cinf64[b:b + 64, h:h + 1], psC[b:b + 64, 0:1]),
                    waits=(("tT", tw),), incs=(("vV", 1),))
                cinf_evac = sch.c("vV")

            for pidx, (h, j) in enumerate(DPAIRS):
                if h // 8 != gv:
                    continue
                hg = h - 8 * gv
                base = 64 * (h % 2)
                tp = (0, 64) if (h % 2) else (0, 0)
                sch.add("sync", dma(widef[:, :], relwin.ap()[pidx]),
                        waits=(("vV", wide_cast_done),), incs=(("dR", 16),))
                dl = sch.c("dR")
                sch.add("vector", lambda e: e.tensor_copy(wideb[:, :], widef[:, :]),
                        waits=(("dR", dl), ("tT", wide_mm_done)),
                        incs=(("vV", 1),))
                wide_cast_done = sch.c("vV")
                for cb in range(FCH):
                    w = []
                    if cb == 0:
                        w = [("vV", wide_cast_done), ("vV", delta_add_done),
                             ("vV", cinf_evac)]
                    sch.add("tensor", mm(psC[base:base + 64, 128:256],
                                         vstage[:, 4160 * hg + 65 * cb:4160 * hg + 65 * cb + 64],
                                         wideb[:, 8 * cb:8 * cb + 128],
                                         start=(cb == 0), stop=(cb == FCH - 1),
                                         tile_position=tp),
                            waits=w, incs=((("tT", 1),) if cb == FCH - 1 else ()))
                wide_mm_done = sch.c("tT")
                sch.add("vector", lambda e, b=base: e.tensor_copy(
                    crev[b:b + 64, 0:128], psC[b:b + 64, 255:127:-1]),
                    waits=(("tT", wide_mm_done),), incs=(("vV", 1),))
                vv = sch.c("vV")
                sch.add("vector", lambda e, b=base, h=h: e.tensor_scalar(
                    crev[b:b + 64, 0:128], crev[b:b + 64, 0:128],
                    cinf64[b:b + 64, h:h + 1], None, ALU.subtract),
                    waits=(("vV", vv),), incs=(("vV", 1),))
                vv2 = sch.c("vV")
                cc6 = h // 2 - 6
                dv = delta[base:base + 64, :].rearrange(
                    "p (c t) -> p c t", c=2, t=TLOC)[:, cc6, j::16]
                sch.add("vector", lambda e, dv=dv, b=base: e.tensor_tensor(
                    dv, dv, crev[b:b + 64, 0:128], ALU.add),
                    waits=(("gG", 1), ("vV", vv2)), incs=(("vV", 1),))
                delta_add_done = sch.c("vV")
        a_done_toks = (("dY0", sch.c("dY0")), ("dY1", sch.c("dY1")),
                       ("dF", sch.c("dF")))

    sch.barrier()

    scope_BD = Scope(nc)
    outT = scope_BD.t("outT", [P, 8 * TLOC], F32)
    with (
        nc.sbuf_tensor("kdup0", [P, 8192], BF16) as kdup0,
        nc.sbuf_tensor("kdup1", [P, 8192], BF16) as kdup1,
        nc.sbuf_tensor("vbuf0", [P, 4160], BF16) as vbuf0,
        nc.sbuf_tensor("vbuf1", [P, 4160], BF16) as vbuf1,
        nc.sbuf_tensor("qdup0", [P, 2048], BF16) as qdup0,
        nc.sbuf_tensor("qdup1", [P, 2048], BF16) as qdup1,
        nc.sbuf_tensor("ebuf", [P, 2048], BF16) as ebuf,
        nc.sbuf_tensor("denrow", [P, 512], F32) as denrow,
        nc.sbuf_tensor("stage64", [P, 512], F32) as stage64,
    ):
        kdup = (kdup0, kdup1)
        vbuf = (vbuf0, vbuf1)
        qdup = (qdup0, qdup1)
        head_bufs_free = [0, 0]

        def load_head(h):
            buf = h % 2
            dH = f"dH{buf}"
            w = [("tT", head_bufs_free[buf])] + list(a_done_toks)
            for half in range(2):
                src = AP(k_s, h * (NPAN_F * 64 * 512),
                         [[512, 64], [64 * 512, 16], [1, 512]])
                sch.add("sync", dma(kdup[buf][64 * half:64 * half + 64, :], src),
                        waits=w, incs=((dH, 16),))
                w = []
            sch.add("sync", dma(vbuf[buf][:, :], v_s.ap()[h]), incs=((dH, 16),))
            for half in range(2):
                src = AP(q_s, h * (NPAN_L * 64 * 512),
                         [[512, 64], [64 * 512, 4], [1, 512]])
                sch.add("sync", dma(qdup[buf][64 * half:64 * half + 64, :], src),
                        incs=((dH, 16),))
            return (dH, sch.c(dH))

        head_ld = [None] * 17
        head_ld[0] = load_head(0)
        G = 0
        evd_count = [0, 0]
        dennat_done = 0
        stage64_free = 0
        for h in range(H):
            buf = h % 2
            if h + 1 < H:
                head_ld[h + 1] = load_head(h + 1)
            for pnl in range(NPAN_L):
                PNL = (h * NPAN_L + pnl) % 2
                evacc = psD[0:65, 0:512] if PNL == 0 else psC[0:65, 0:512]

                def emit_ev(pend, last):
                    half, pc0, pc1, Gv, first = pend
                    wev = [("ex", Gv + 1)]
                    if first:
                        wev.append(("vV", evd_count[PNL]))
                    for ci, c in enumerate((pc0, pc1)):
                        sch.add("tensor", mm(evacc,
                                             vbuf[buf][:, 65 * c:65 * c + 65],
                                             ebuf[:, half * 1024 + 512 * ci:
                                                  half * 1024 + 512 * (ci + 1)],
                                             start=(first and ci == 0),
                                             stop=(last and ci == 1)),
                                waits=(wev if ci == 0 else []),
                                incs=((("ev", 1),) if ci == 1 else ()))
                pend = None
                for k in range(32):
                    ps = (psA, psB)[G % 2]
                    c0, c1 = 2 * k, 2 * k + 1
                    w = []
                    if k == 0 and pnl == 0:
                        w.append(head_ld[h])
                    if G >= 2:
                        w.append(("ex", G - 1))
                    sch.add("tensor", mm(ps[:, 0:512],
                                         kdup[buf][0:64, 128 * c0:128 * (c0 + 1)],
                                         qdup[buf][0:64, 512 * pnl:512 * (pnl + 1)],
                                         start=True, stop=True, tile_position=(0, 0)),
                            waits=w)
                    sch.add("tensor", mm(ps[:, 512:1024],
                                         kdup[buf][64:128, 128 * c1:128 * (c1 + 1)],
                                         qdup[buf][64:128, 512 * pnl:512 * (pnl + 1)],
                                         start=True, stop=True, tile_position=(64, 0)),
                            incs=(("qk", 1),))
                    if pend is not None:
                        emit_ev(pend, last=False)
                    w = [("qk", G + 1)]
                    if G >= 2:
                        w.append(("ev", G - 1))
                    sch.add("scalar", lambda e, ps=ps, o=(G % 2) * 1024: e.activation(
                        ebuf[:, o:o + 1024], ps[:, :], AF.Exp),
                        waits=w, incs=(("ex", 1),))
                    pend = (G % 2, c0, c1, G, k == 0)
                    G += 1
                emit_ev(pend, last=True)
                evw = sch.c("ev")
                cc = h // 2
                colofs = TLOC * cc + 512 * pnl
                evtop = psD[0:64, 0:512] if PNL == 0 else psC[0:64, 0:512]
                evden = psD[64:65, 0:512] if PNL == 0 else psC[64:65, 0:512]
                if h % 2 == 0:
                    sch.add("vector", lambda e, o=colofs, s=evtop: e.tensor_copy(
                        outT[0:64, o:o + 512], s),
                        waits=(("ev", evw),), incs=(("vV", 1),))
                else:
                    sch.add("vector", lambda e, s=evtop: e.tensor_copy(
                        stage64[0:64, :], s),
                        waits=(("ev", evw), ("dB", stage64_free)), incs=(("vV", 1),))
                    vv = sch.c("vV")
                    sch.add("gpsimd", dma(outT[64:128, colofs:colofs + 512],
                                          stage64[0:64, :]),
                            waits=(("vV", vv),), incs=(("dB", 16),))
                    stage64_free = sch.c("dB")
                sch.add("vector", lambda e, s=evden: e.tensor_copy(
                    denrow[64:65, :], s), incs=(("vV", 1),))
                denw = sch.c("vV")
                for jj in range(4):
                    w = []
                    if jj == 0:
                        w = [("vV", denw), ("vV", dennat_done)]
                    sch.add("tensor", lambda e, jj=jj: e.transpose(
                        psC[:, 640 + jj:641 + jj], denrow[64:65, 128 * jj:128 * (jj + 1)],
                        identf[64:65, 64:65]),
                        waits=w, incs=((("tT", 1),) if jj == 3 else ()))
                tw = sch.c("tT")
                dst = den_nat[:, :].rearrange("p (t h) -> p t h", t=16, h=16)[:, 4 * pnl:4 * pnl + 4, h]
                sch.add("vector", lambda e, d=dst: e.tensor_copy(d, psC[:, 640:644]),
                        waits=(("tT", tw),), incs=(("vV", 1),))
                dennat_done = sch.c("vV")
                evd_count[PNL] = sch.c("vV")
            head_bufs_free[buf] = sch.c("tT")
        b_done_dG = sch.c("dB")

    sch.barrier()

    with (
        nc.sbuf_tensor("natx", [P, 1024], F32) as natx,
        nc.sbuf_tensor("dnat", [P, 256], F32) as dnat,
        nc.sbuf_tensor("rdn", [P, 16], F32) as rdn,
        nc.sbuf_tensor("gamb", [P, 1024], F32) as gamb,
        nc.sbuf_tensor("betb", [P, 1024], F32) as betb,
        nc.sbuf_tensor("cinfb", [P, 1024], F32) as cinfb,
        nc.sbuf_tensor("cirow", [32, 128], F32) as cirow,
        nc.sbuf_tensor("selrow", [1, 128], F32) as selrow,
        nc.sbuf_tensor("xres", [P, 1024], F32) as xres,
        nc.sbuf_tensor("x1t", [P, 1024], F32) as x1t,
        nc.sbuf_tensor("t3", [P, 1024], F32) as t3,
        nc.sbuf_tensor("sqscrD", [P, 1024], F32) as sqscrD,
        nc.sbuf_tensor("sumsD", [P, 8], F32) as sumsD,
        nc.sbuf_tensor("rstdD", [P, 1], F32) as rstdD,
        nc.sbuf_tensor("nmrD", [P, 1], F32) as nmrD,
    ):
        if debug:
            sch.add("sync", dma(dbg_st.ap(), dbgst[:, :]), incs=(("dDbg", 16),))
            sch.add("sync", dma(dbg_outT.ap(), outT[:, :]), incs=(("dDbg", 16),))
            sch.add("sync", dma(dbg_den.ap(), den_nat[:, :]), incs=(("dDbg", 16),))
            sch.add("sync", dma(dbg_delta.ap(), delta[:, :]), incs=(("dDbg", 16),))
            sch.add("sync", dma(dbg_cinf.ap(), cinf64[:, :]), incs=(("dDbg", 16),))
        sch.add("sync", dma(gamb[:, :], gammab_d.ap()), incs=(("dC1", 16),))
        sch.add("sync", dma(betb[:, :], betab_d.ap()), incs=(("dC1", 16),))
        gb_ld_d = sch.c("dC1")
        sch.add("tensor", lambda e: e.transpose(psC[0:16, 0:128], cinf64[:, :], identf[:, :]),
                incs=(("tT", 1),))
        tw = sch.c("tT")
        sch.add("vector", lambda e: e.memset(cirow[:, :], 0.0), incs=(("vV", 1),))
        mz = sch.c("vV")
        sch.add("vector", lambda e: e.tensor_copy(cirow[0:16, :], psC[0:16, 0:128]),
                waits=(("tT", tw), ("vV", mz)), incs=(("vV", 1),))
        cirow_w = sch.c("vV")
        row1 = cirow
        for h in range(H):
            sch.add("tensor", mm(psD[0:1, 0:128], identf[0:32, h:h + 1], cirow[:, :],
                                 start=True, stop=True),
                    waits=(([("vV", cirow_w), ("tT", tw)] if h == 0 else []) +
                           ([("vV", cinfb_evac)] if h > 0 else [])),
                    incs=(("tT", 1),))
            tw2 = sch.c("tT")
            sch.add("vector", lambda e, h=h: e.tensor_copy(selrow[0:1, :], psD[0:1, 0:128]),
                    waits=(("tT", tw2),), incs=(("vV", 1),))
            sw = sch.c("vV")
            sch.add("tensor", lambda e, h=h: e.matmul(
                psD[:, 128:192], ones1[:, :], selrow[0:1, 64 * (h % 2):64 * (h % 2) + 64],
                start=True, stop=True),
                waits=(("vV", sw),), incs=(("tT", 1),))
            tw3 = sch.c("tT")
            sch.add("vector", lambda e, h=h: e.tensor_copy(
                cinfb[:, 64 * h:64 * (h + 1)], psD[:, 128:192]),
                waits=(("tT", tw3),), incs=(("vV", 1),))
            cinfb_evac = sch.c("vV")
        bcast_ready_v = sch.c("vV")

        d_evac_done = 0
        xres_free = 0
        x1_st_done = 0
        for tt in range(NT_L):
            for i in range(8):
                w = []
                if i == 0:
                    w = [("vV", d_evac_done), ("dB", b_done_dG)]
                sch.add("tensor", lambda e, i=i, tt=tt: e.transpose(
                    psC[:, 128 * i:128 * (i + 1)],
                    outT[:, TLOC * i + 128 * tt:TLOC * i + 128 * (tt + 1)], identf[:, :]),
                    waits=w)
            for i in range(2):
                sch.add("tensor", lambda e, i=i, tt=tt: e.transpose(
                    psD[:, 128 * i:128 * (i + 1)],
                    delta[:, TLOC * i + 128 * tt:TLOC * i + 128 * (tt + 1)], identf[:, :]),
                    incs=((("tT", 1),) if i == 1 else ()))
            tw = sch.c("tT")
            sch.add("vector", lambda e: e.tensor_copy(natx[:, :], psC[:, 0:1024]),
                    waits=(("tT", tw),))
            sch.add("vector", lambda e: e.tensor_copy(dnat[:, :], psD[:, 0:256]),
                    incs=(("vV", 1),))
            d_evac_done = sch.c("vV")
            sch.add("vector", lambda e, tt=tt: e.reciprocal(
                rdn[:, :], den_nat[:, 16 * tt:16 * (tt + 1)]), incs=(("vV", 1),))
            vw = sch.c("vV")
            for hh in range(H):
                w = [("vV", vw)] if hh == 0 else []
                sch.add("vector", lambda e, hh=hh: e.tensor_scalar(
                    natx[:, 64 * hh:64 * (hh + 1)], natx[:, 64 * hh:64 * (hh + 1)],
                    rdn[:, hh:hh + 1], None, ALU.mult), waits=w)
            sch.add("vector", lambda e: e.tensor_tensor(
                natx[:, :], natx[:, :], cinfb[:, :], ALU.add),
                waits=(("vV", bcast_ready_v),))
            sch.add("vector", lambda e: e.tensor_tensor(
                natx[:, 768:1024], natx[:, 768:1024], dnat[:, :], ALU.add))
            vwst = emit_stats(natx[:, :], DM, EPS_ATT, rstdD[:, 0:1], nmrD[:, 0:1],
                              sumsD, sqscrD)
            sch.add("vector", lambda e: e.tensor_scalar(
                t3[:, :], natx[:, :], rstdD[:, 0:1], nmrD[:, 0:1], ALU.mult, ALU.add),
                waits=(("vV", vwst),))
            sch.add("vector", lambda e: e.tensor_tensor(t3[:, :], t3[:, :], gamb[:, :], ALU.mult),
                    waits=(("dC1", gb_ld_d),))
            sch.add("vector", lambda e: e.tensor_tensor(t3[:, :], t3[:, :], betb[:, :], ALU.add))
            vwst = emit_stats(t3[:, :], DM, EPS_LN, rstdD[:, 0:1], nmrD[:, 0:1],
                              sumsD, sqscrD)
            sch.add("vector", lambda e: e.tensor_scalar(
                t3[:, :], t3[:, :], rstdD[:, 0:1], nmrD[:, 0:1], ALU.mult, ALU.add),
                waits=(("vV", vwst),), incs=(("vV", 1),))
            src = AP(x_loc, 128 * tt * DM, [[DM, P], [1, DM]])
            sch.add("gpsimd", dma(xres[:, :], src),
                    waits=(("vV", xres_free),), incs=(("dG", 16),))
            dw = sch.c("dG")
            sch.add("vector", lambda e: e.tensor_tensor(x1t[:, :], t3[:, :], xres[:, :], ALU.add),
                    waits=(("dG", dw), ("dZ0", x1_st_done)), incs=(("vV", 1),))
            xres_free = sch.c("vV")
            vw = sch.c("vV")
            dst = AP(x1_s, 128 * tt * DM, [[DM, P], [1, DM]])
            sch.add("sync", dma(dst, x1t[:, :]), waits=(("vV", vw),), incs=(("dZ0", 16),))
            x1_st_done = sch.c("dZ0")
        d_x1_tok = ("dZ0", sch.c("dZ0"))
    scope_BD.close()
    scope_AD.close()

    sch.barrier()

    with (
        nc.sbuf_tensor("h2T", [P, 8 * TLOC], BF16) as h2T,
        nc.sbuf_tensor("gateT", [32, TLOC], F32) as gateT,
    ):
        scE13 = Scope(nc)
        hT = scE13.t("hT", [P, 8 * TLOC], BF16)
        wq2b = scE13.t("wq2b", [P, 8192], BF16)
        wdnb = scE13.t("wdnb", [P, 512], BF16)
        wupb = scE13.t("wupb", [64, 2048], BF16)
        wob = scE13.t("wob", [P, 8192], BF16)
        wgb = scE13.t("wgb", [P, 64], BF16)
        wstgE0 = scE13.t("wstgE0", [P, 2048], F32)
        wstgE1 = scE13.t("wstgE1", [P, 2048], F32)
        g1b = scE13.t("g1b", [P, 1024], F32)
        g2b = scE13.t("g2b", [P, 1024], F32)
        bqb = scE13.t("bqb", [P, 1024], F32)
        bdnb = scE13.t("bdnb", [P, 64], F32)
        bupb = scE13.t("bupb", [P, 2048], F32)
        bob = scE13.t("bob", [P, 1024], F32)
        bgb = scE13.t("bgb", [P, 8], F32)
        x1til0 = scE13.t("x1til0", [P, 1024], F32)
        x1til1 = scE13.t("x1til1", [P, 1024], F32)
        ut0 = scE13.t("ut0", [P, 1024], F32)
        ut1 = scE13.t("ut1", [P, 1024], F32)
        hb = scE13.t("hb", [P, 1024], BF16)
        h2b = scE13.t("h2b", [P, 1024], BF16)
        qt0 = scE13.t("qt0", [P, 1024], BF16)
        qt1 = scE13.t("qt1", [P, 1024], BF16)
        znat = scE13.t("znat", [P, 64], BF16)
        zT = scE13.t("zT", [64, 128], BF16)
        kvt0 = scE13.t("kvt0", [P, 2048], BF16)
        kvt1 = scE13.t("kvt1", [P, 2048], BF16)
        sumsG = scE13.t("sumsG", [P, 8], F32)
        attL = scE13.t("attL", [P, 256], F32)
        attM = scE13.t("attM", [P, 16], F32)
        attS = scE13.t("attS", [P, 16], F32)
        avacc = scE13.t("avacc", [P, 1024], F32)
        avtmp0 = scE13.t("avtmp0", [P, 1024], F32)
        avtmp1 = scE13.t("avtmp1", [P, 1024], F32)
        avb = scE13.t("avb", [P, 1024], BF16)
        avT = scE13.t("avT", [P, 1024], BF16)
        v1til = scE13.t("v1til", [P, 1024], F32)
        gtile = scE13.t("gtile", [P, 8], F32)
        sumsE = scE13.t("sumsE", [P, 8], F32)
        rstdE = scE13.t("rstdE", [P, 1], F32)
        r2E = scE13.t("r2E", [P, 1], F32)
        nmrE = scE13.t("nmrE", [P, 1], F32)
        if True:
            x1til = (x1til0, x1til1)
            wstgE = (wstgE0, wstgE1)
            wst_freeE = [0, 0]

            def loadw(wdram, wbf, ncols, scale, rows=DM):
                npc = max(1, 2048 // ncols)
                nchunk = rows // P
                for pi, piece in enumerate(range(0, nchunk, npc)):
                    bi = pi % 2
                    stg = wstgE[bi]
                    np_ = min(npc, nchunk - piece)
                    src = AP(wdram, piece * P * ncols,
                             [[ncols, P], [P * ncols, np_], [1, ncols]])
                    sch.add("sync", dma(stg[:, 0:np_ * ncols], src),
                            waits=(("vV", wst_freeE[bi]),), incs=((f"dW{bi}", 16),))
                    dl = sch.c(f"dW{bi}")
                    sch.add("vector", lambda e, s=stg, n=np_ * ncols, d=wbf,
                            o=piece * ncols, sc=scale, bi=bi: e.tensor_scalar(
                                d[:, o:o + n], s[:, 0:n], sc, None, ALU.mult),
                            waits=((f"dW{bi}", dl),), incs=(("vV", 1),))
                    wst_freeE[bi] = sch.c("vV")

            sch.add("vector", lambda e: e.memset(gateT[:, :], 0.0), incs=(("vV", 1),))
            loadw(f_wq, wq2b, DM, 0.125)
            loadw(f_wdn, wdnb, L_FF, 1.0)
            loadw(f_wo, wob, DM, 1.0)
            loadw(f_wg, wgb, NEXP, 1.0)
            sch.add("sync", dma(wstgE0[:, :], AP(f_wup, 0, [[2048, P], [1, 2048]])),
                    waits=(("vV", wst_freeE[0]),), incs=(("dW0", 16),))
            dl = sch.c("dW0")
            sch.add("vector", lambda e: e.tensor_scalar(
                wupb[:, :], wstgE0[0:64, 0:2048], 1.0, None, ALU.mult),
                waits=(("dW0", dl),), incs=(("vV", 1),))
            wst_freeE[0] = sch.c("vV")

            for (bt, bd) in ((g1b, g1b_d), (g2b, g2b_d), (bqb, bqb_d), (bdnb, bdnb_d),
                             (bupb, bupb_d), (bob, bob_d), (bgb, bgb_d)):
                sch.add("sync", dma(bt[:, :], bd.ap()), incs=(("dC1", 16),))
            wready = sch.c("vV")
            bias_ld = sch.c("dC1")

            x1_free = [0, 0]
            psab_free = [0, 0]
            psD_free = 0
            psT_free = 0
            v1_st_tok = ("dZ0", sch.c("dZ0"))
            v1_war = sch.c("dZ0")
            gate_done = 0
            avacc_free = 0
            avtmp_free = [0, 0]
            av_gps_done = sch.c("gG")
            utb = (ut0, ut1)
            qtb = (qt0, qt1)
            kvtb = (kvt0, kvt1)
            kv_readyP = [0, 0]

            def emit_pre(tt):
                nonlocal psT_free, psD_free, av_gps_done
                bi = tt % 2
                ut, qt, kvt = utb[bi], qtb[bi], kvtb[bi]
                src = AP(x1_s, 128 * tt * DM, [[DM, P], [1, DM]])
                w = [("vV", x1_free[bi]), d_x1_tok, ("vV", wready)]
                if debug and tt == 0:
                    w.append(("dDbg", sch.c("dDbg")))
                sch.add("sync", dma(x1til[bi][:, :], src),
                        waits=tuple(w), incs=((f"dX{bi}", 16),))
                x1l = sch.c(f"dX{bi}")
                vw = emit_stats(x1til[bi][:, :], DM, EPS_LN, rstdE[:, 0:1], nmrE[:, 0:1],
                                sumsE, avacc, first_waits=((f"dX{bi}", x1l),))
                sch.add("vector", lambda e, bi=bi, ut=ut: e.tensor_scalar(
                    ut[:, :], x1til[bi][:, :], rstdE[:, 0:1], nmrE[:, 0:1],
                    ALU.mult, ALU.add), waits=(("vV", vw),), incs=(("vV", 1),))
                x1_free[bi] = sch.c("vV")
                sch.add("vector", lambda e: e.tensor_tensor(
                    sumsE[:, 7:8], rstdE[:, 0:1], rstdE[:, 0:1], ALU.mult),
                    waits=(("vV", vw),), incs=(("vV", 1),))
                vw2 = sch.c("vV")
                sch.add("vector", lambda e: e.tensor_tensor(
                    sumsE[:, 7:8], sumsE[:, 7:8], sumsE[:, 5:6], ALU.mult),
                    waits=(("vV", vw2),), incs=(("vV", 1),))
                vw2 = sch.c("vV")
                sch.add("scalar", lambda e: e.activation(
                    sumsE[:, 7:8], sumsE[:, 7:8], AF.Sqrt, bias=cst[:, 2:3], scale=1.0),
                    waits=(("vV", vw2),), incs=(("aA", 1),))
                aw = sch.c("aA")
                sch.add("vector", lambda e: e.reciprocal(r2E[:, 0:1], sumsE[:, 7:8]),
                        waits=(("aA", aw),), incs=(("vV", 1),))
                vw3 = sch.c("vV")
                sch.add("vector", lambda e: e.tensor_scalar(
                    hb[:, :], ut[:, :], r2E[:, 0:1], None, ALU.mult),
                    waits=(("vV", vw3), ("dC1", bias_ld)))
                sch.add("vector", lambda e: e.tensor_tensor(
                    hb[:, :], hb[:, :], g1b[:, :], ALU.mult), incs=(("vV", 1),))
                hbw = sch.c("vV")
                for i in range(8):
                    w = [("vV", hbw), ("vV", psT_free)] if i == 0 else []
                    sch.add("tensor", lambda e, i=i: e.transpose(
                        psT[:, 128 * i:128 * (i + 1)], hb[:, 128 * i:128 * (i + 1)],
                        identb[:, :]),
                        waits=w, incs=((("tT", 1),) if i == 7 else ()))
                tw = sch.c("tT")
                dstT = hT[:, :].rearrange("p (c t) -> p c t", c=8, t=TLOC)[:, :, 128 * tt:128 * (tt + 1)]
                sch.add("vector", lambda e, d=dstT: e.tensor_copy(
                    d, psT[:, :].rearrange("p (c t) -> p c t", c=8, t=128)),
                    waits=(("tT", tw),), incs=(("vV", 1),))
                psT_free = sch.c("vV")
                hT_ready = sch.c("vV")

                for dh in range(2):
                    ps = (psA, psB)[dh]
                    for cc in range(8):
                        w = []
                        if cc == 0:
                            w = [("vV", hT_ready), ("vV", wready), ("vV", psab_free[dh])]
                        sch.add("tensor", mm(
                            ps[:, 0:512],
                            hT[:, TLOC * cc + 128 * tt:TLOC * cc + 128 * (tt + 1)],
                            wq2b[:, 1024 * cc + 512 * dh:1024 * cc + 512 * (dh + 1)],
                            start=(cc == 0), stop=(cc == 7)),
                            waits=w, incs=((("tT", 1),) if cc == 7 else ()))
                    tw = sch.c("tT")
                    sch.add("vector", lambda e, dh=dh, ps=ps: e.scalar_tensor_tensor(
                        out=qt[:, 512 * dh:512 * (dh + 1)], in0=bqb[:, 512 * dh:512 * (dh + 1)],
                        scalar=0.125, in1=ps[:, 0:512], op0=ALU.mult, op1=ALU.add),
                        waits=(("tT", tw),), incs=(("vV", 1),))
                    psab_free[dh] = sch.c("vV")
                for cc in range(8):
                    w = [("vV", psD_free), ("vV", hT_ready)] if cc == 0 else []
                    sch.add("tensor", mm(psD[:, 0:64],
                                         hT[:, TLOC * cc + 128 * tt:TLOC * cc + 128 * (tt + 1)],
                                         wdnb[:, 64 * cc:64 * (cc + 1)],
                                         start=(cc == 0), stop=(cc == 7)),
                            waits=w, incs=((("tT", 1),) if cc == 7 else ()))
                tw = sch.c("tT")
                sch.add("vector", lambda e: e.scalar_tensor_tensor(
                    out=znat[:, :], in0=bdnb[:, 0:64], scalar=1.0, in1=psD[:, 0:64],
                    op0=ALU.mult, op1=ALU.add),
                    waits=(("tT", tw),), incs=(("vV", 1),))
                zw = sch.c("vV")
                psD_free = sch.c("vV")
                sch.add("tensor", lambda e: e.transpose(
                    psT[0:64, 0:128], znat[:, :], identb[:, :]),
                    waits=(("vV", zw), ("vV", psT_free)), incs=(("tT", 1),))
                tw = sch.c("tT")
                sch.add("vector", lambda e: e.tensor_copy(zT[:, :], psT[0:64, 0:128]),
                        waits=(("tT", tw),), incs=(("vV", 1),))
                zTw = sch.c("vV")
                psT_free = sch.c("vV")
                for kk4 in range(4):
                    ps = (psA, psB)[kk4 % 2]
                    w = [("vV", psab_free[kk4 % 2])]
                    if kk4 == 0:
                        w.append(("vV", zTw))
                        w.append(("gG", av_gps_done))
                    sch.add("tensor", mm(ps[:, 0:512], zT[:, :],
                                         wupb[:, 512 * kk4:512 * (kk4 + 1)],
                                         start=True, stop=True),
                            waits=w, incs=(("tT", 1),))
                    tw = sch.c("tT")
                    sch.add("vector", lambda e, kk4=kk4, ps=ps: e.scalar_tensor_tensor(
                        out=kvt[:, 512 * kk4:512 * (kk4 + 1)],
                        in0=bupb[:, 512 * kk4:512 * (kk4 + 1)], scalar=1.0,
                        in1=ps[:, 0:512], op0=ALU.mult, op1=ALU.add),
                        waits=(("tT", tw),), incs=(("vV", 1),))
                    psab_free[kk4 % 2] = sch.c("vV")
                kv_ready = sch.c("vV")

                kv_readyP[bi] = sch.c("vV")

            def emit_post(tt):
                nonlocal psT_free, psD_free, av_gps_done, avacc_free, v1_war, v1_st_tok, gate_done
                bi = tt % 2
                ut, qt, kvt = utb[bi], qtb[bi], kvtb[bi]
                avtmp = (avtmp0, avtmp1)
                qr3 = qt[:, :].rearrange("p (h d) -> p h d", h=16, d=64)
                for gg in range(H):
                    tb = avtmp[gg % 2]
                    in1 = kvt[:, 128 * gg:128 * gg + 64].unsqueeze(1).broadcast_to(
                        [P, 16, 64])
                    w = [("vV", avtmp_free[gg % 2])]
                    if gg == 0:
                        w.append(("vV", kv_readyP[bi]))
                    sch.add("gpsimd", lambda e, tb=tb, in1=in1: e.tensor_tensor(
                        tb[:, :].rearrange("p (h d) -> p h d", h=16, d=64),
                        qr3, in1, ALU.mult),
                        waits=w, incs=(("gG", 1),))
                    gw = sch.c("gG")
                    dstc = attL[:, :].rearrange("p (h g) -> p h g", h=16, g=16)[:, :, gg]
                    sch.add("vector", lambda e, tb=tb, d=dstc: e.reduce_sum(
                        d, tb[:, :].rearrange("p (h d) -> p h d", h=16, d=64),
                        axis=AX.X),
                        waits=(("gG", gw),), incs=(("vV", 1),))
                    avtmp_free[gg % 2] = sch.c("vV")
                sch.add("vector", lambda e: e.reduce_max(
                    attM[:, :], attL[:, :].rearrange("p (h g) -> p h g", h=16, g=16),
                    axis=AX.X), waits=(("vV", sch.c("vV")),), incs=(("vV", 1),))
                sch.add("vector", None, waits=(("vV", sch.c("vV")),))
                sch.add("vector", lambda e: e.tensor_tensor(
                    attL[:, :].rearrange("p (h g) -> p h g", h=16, g=16),
                    attL[:, :].rearrange("p (h g) -> p h g", h=16, g=16),
                    attM[:, :].unsqueeze(2).broadcast_to([P, 16, 16]),
                    ALU.subtract), incs=(("vV", 1),))
                vw = sch.c("vV")
                sch.add("scalar", lambda e: e.activation(attL[:, :], attL[:, :], AF.Exp),
                        waits=(("vV", vw),), incs=(("aA", 1),))
                aw = sch.c("aA")
                sch.add("vector", lambda e: e.reduce_sum(
                    attS[:, :], attL[:, :].rearrange("p (h g) -> p h g", h=16, g=16),
                    axis=AX.X), waits=(("aA", aw),), incs=(("vV", 1),))
                sch.add("vector", None, waits=(("vV", sch.c("vV")),))
                sch.add("vector", lambda e: e.reciprocal(attS[:, :], attS[:, :]),
                        incs=(("vV", 1),))
                sch.add("vector", None, waits=(("vV", sch.c("vV")),))
                sch.add("vector", lambda e: e.tensor_tensor(
                    attL[:, :].rearrange("p (h g) -> p h g", h=16, g=16),
                    attL[:, :].rearrange("p (h g) -> p h g", h=16, g=16),
                    attS[:, :].unsqueeze(2).broadcast_to([P, 16, 16]),
                    ALU.mult), incs=(("vV", 1),))
                attw = sch.c("vV")
                for gg in range(H):
                    tb = avtmp[gg % 2]
                    in0 = attL[:, :].rearrange("p (h g) -> p h g", h=16, g=16)[
                        :, :, gg].unsqueeze(2).broadcast_to([P, 16, 64])
                    in1 = kvt[:, 128 * gg + 64:128 * (gg + 1)].unsqueeze(1).broadcast_to(
                        [P, 16, 64])
                    w = [("vV", avtmp_free[gg % 2])]
                    if gg == 0:
                        w.append(("vV", attw))
                    sch.add("gpsimd", lambda e, tb=tb, in0=in0, in1=in1: e.tensor_tensor(
                        tb[:, :].rearrange("p (h d) -> p h d", h=16, d=64),
                        in0, in1, ALU.mult),
                        waits=w, incs=(("gG", 1),))
                    gw = sch.c("gG")
                    if gg == 0:
                        sch.add("vector", lambda e, tb=tb: e.tensor_copy(
                            avacc[:, :], tb[:, :]),
                            waits=(("gG", gw), ("vV", avacc_free)), incs=(("vV", 1),))
                    else:
                        sch.add("vector", lambda e, tb=tb: e.tensor_tensor(
                            avacc[:, :], avacc[:, :], tb[:, :], ALU.add),
                            waits=(("gG", gw),), incs=(("vV", 1),))
                    avtmp_free[gg % 2] = sch.c("vV")
                av_gps_done = sch.c("gG")
                vlast = sch.c("vV")
                sch.add("vector", lambda e: e.tensor_copy(avb[:, :], avacc[:, :]),
                        waits=(("vV", vlast),), incs=(("vV", 1),))
                avacc_free = sch.c("vV")
                avbw = sch.c("vV")
                for i in range(8):
                    w = [("vV", avbw), ("vV", psT_free)] if i == 0 else []
                    sch.add("tensor", lambda e, i=i: e.transpose(
                        psT[:, 128 * i:128 * (i + 1)], avb[:, 128 * i:128 * (i + 1)],
                        identb[:, :]),
                        waits=w, incs=((("tT", 1),) if i == 7 else ()))
                tw = sch.c("tT")
                sch.add("vector", lambda e: e.tensor_copy(avT[:, :], psT[:, :]),
                        waits=(("tT", tw),), incs=(("vV", 1),))
                psT_free = sch.c("vV")
                avTw = sch.c("vV")
                for dh in range(2):
                    ps = (psA, psB)[dh]
                    for cc in range(8):
                        w = []
                        if cc == 0:
                            w = [("vV", avTw), ("vV", psab_free[dh])]
                        sch.add("tensor", mm(
                            ps[:, 0:512], avT[:, 128 * cc:128 * (cc + 1)],
                            wob[:, 1024 * cc + 512 * dh:1024 * cc + 512 * (dh + 1)],
                            start=(cc == 0), stop=(cc == 7)),
                            waits=w, incs=((("tT", 1),) if cc == 7 else ()))
                    tw = sch.c("tT")
                    wv1 = [("tT", tw)]
                    if dh == 0:
                        wv1.append(("dZ0", v1_war))
                    sch.add("vector", lambda e, dh=dh, ps=ps: e.scalar_tensor_tensor(
                        out=v1til[:, 512 * dh:512 * (dh + 1)],
                        in0=bob[:, 512 * dh:512 * (dh + 1)], scalar=1.0,
                        in1=ps[:, 0:512], op0=ALU.mult, op1=ALU.add),
                        waits=tuple(wv1))
                    sch.add("vector", lambda e, dh=dh: e.tensor_tensor(
                        v1til[:, 512 * dh:512 * (dh + 1)], v1til[:, 512 * dh:512 * (dh + 1)],
                        ut[:, 512 * dh:512 * (dh + 1)], ALU.add), incs=(("vV", 1),))
                    psab_free[dh] = sch.c("vV")
                v1w = sch.c("vV")
                dst = AP(v1_s, 128 * tt * DM, [[DM, P], [1, DM]])
                sch.add("sync", dma(dst, v1til[:, :]),
                        waits=(("vV", v1w),), incs=(("dZ0", 16),))
                v1_war = sch.c("dZ0")
                v1_st_tok = ("dZ0", sch.c("dZ0"))
                sch.add("vector", lambda e: e.scalar_tensor_tensor(
                    out=avacc[:, :], in0=v1til[:, :], scalar=1.0, in1=v1til[:, :],
                    op0=ALU.mult, op1=ALU.mult, accum_out=sumsG[:, 0:1]),
                    incs=(("vV", 1),))
                vw = sch.c("vV")
                sch.add("vector", lambda e: e.tensor_scalar(
                    sumsG[:, 1:2], sumsG[:, 0:1], 1.0 / DM, None, ALU.mult),
                    waits=(("vV", vw),), incs=(("vV", 1),))
                vw = sch.c("vV")
                sch.add("scalar", lambda e: e.activation(
                    sumsG[:, 2:3], sumsG[:, 1:2], AF.Sqrt, bias=cst[:, 2:3], scale=1.0),
                    waits=(("vV", vw),), incs=(("aA", 1),))
                aw = sch.c("aA")
                sch.add("vector", lambda e: e.reciprocal(sumsG[:, 3:4], sumsG[:, 2:3]),
                        waits=(("aA", aw),), incs=(("vV", 1),))
                vw = sch.c("vV")
                sch.add("vector", lambda e: e.tensor_scalar(
                    h2b[:, :], v1til[:, :], sumsG[:, 3:4], None, ALU.mult),
                    waits=(("vV", vw),))
                sch.add("vector", lambda e: e.tensor_tensor(
                    h2b[:, :], h2b[:, :], g2b[:, :], ALU.mult), incs=(("vV", 1),))
                h2w = sch.c("vV")
                for i in range(8):
                    w = [("vV", h2w), ("vV", psT_free)] if i == 0 else []
                    sch.add("tensor", lambda e, i=i: e.transpose(
                        psT[:, 128 * i:128 * (i + 1)], h2b[:, 128 * i:128 * (i + 1)],
                        identb[:, :]),
                        waits=w, incs=((("tT", 1),) if i == 7 else ()))
                tw = sch.c("tT")
                dstT = h2T[:, :].rearrange("p (c t) -> p c t", c=8, t=TLOC)[:, :, 128 * tt:128 * (tt + 1)]
                sch.add("vector", lambda e, d=dstT: e.tensor_copy(
                    d, psT[:, :].rearrange("p (c t) -> p c t", c=8, t=128)),
                    waits=(("tT", tw),), incs=(("vV", 1),))
                psT_free = sch.c("vV")
                h2T_ready = sch.c("vV")
                for cc in range(8):
                    w = [("vV", h2T_ready), ("vV", psD_free)] if cc == 0 else []
                    sch.add("tensor", mm(psD[:, 0:8],
                                         h2T[:, TLOC * cc + 128 * tt:TLOC * cc + 128 * (tt + 1)],
                                         wgb[:, 8 * cc:8 * (cc + 1)],
                                         start=(cc == 0), stop=(cc == 7)),
                            waits=w, incs=((("tT", 1),) if cc == 7 else ()))
                tw = sch.c("tT")
                sch.add("vector", lambda e: e.scalar_tensor_tensor(
                    out=gtile[:, :], in0=bgb[:, :], scalar=1.0, in1=psD[:, 0:8],
                    op0=ALU.mult, op1=ALU.add), waits=(("tT", tw),), incs=(("vV", 1),))
                psD_free = sch.c("vV")
                sch.add("vector", lambda e: e.reduce_max(
                    sumsG[:, 4:5], gtile[:, :], axis=AX.X),
                    waits=(("vV", psD_free),), incs=(("vV", 1),))
                vw = sch.c("vV")
                sch.add("vector", lambda e: e.tensor_scalar(
                    gtile[:, :], gtile[:, :], sumsG[:, 4:5], None, ALU.subtract),
                    waits=(("vV", vw),), incs=(("vV", 1),))
                vw = sch.c("vV")
                sch.add("scalar", lambda e: e.activation(gtile[:, :], gtile[:, :], AF.Exp),
                        waits=(("vV", vw),), incs=(("aA", 1),))
                aw = sch.c("aA")
                sch.add("vector", lambda e: e.reduce_sum(
                    sumsG[:, 5:6], gtile[:, :], axis=AX.X), waits=(("aA", aw),),
                    incs=(("vV", 1),))
                sch.add("vector", None, waits=(("vV", sch.c("vV")),))
                sch.add("vector", lambda e: e.reciprocal(sumsG[:, 6:7], sumsG[:, 5:6]),
                        incs=(("vV", 1),))
                vw = sch.c("vV")
                sch.add("vector", lambda e: e.tensor_scalar(
                    gtile[:, :], gtile[:, :], sumsG[:, 6:7], None, ALU.mult),
                    waits=(("vV", vw),), incs=(("vV", 1),))
                gw = sch.c("vV")
                sch.add("tensor", lambda e: e.transpose(
                    psC[0:8, 640:768], gtile[:, :], identf[:, :]),
                    waits=(("vV", gw), ("vV", gate_done)), incs=(("tT", 1),))
                tw = sch.c("tT")
                sch.add("vector", lambda e, tt=tt: e.tensor_copy(
                    gateT[0:8, 128 * tt:128 * (tt + 1)], psC[0:8, 640:768]),
                    waits=(("tT", tw),), incs=(("vV", 1),))
                gate_done = sch.c("vV")

            emit_pre(0)
            emit_pre(1)
            for tt in range(NT_L):
                emit_post(tt)
                if tt + 2 < NT_L:
                    emit_pre(tt + 2)
        scE13.close()

        sch.barrier()

        scE5 = Scope(nc)
        gbce_all = scE5.t("gbce_all", [P, 8192], F32)
        selrowE = scE5.t("selrowE", [1, 1024], F32)
        xeh = scE5.t("xeh", [P, 8192], BF16)
        westg0 = scE5.t("westg0", [P, 2048], F32)
        westg1 = scE5.t("westg1", [P, 2048], F32)
        webf0 = scE5.t("webf0", [P, 4096], BF16)
        webf1 = scE5.t("webf1", [P, 4096], BF16)
        combT = scE5.t("combT", [P, 8192], F32)
        cnat = scE5.t("cnat", [P, 1024], F32)
        v1r = scE5.t("v1r", [P, 1024], F32)
        x1r = scE5.t("x1r", [P, 1024], F32)
        fot = scE5.t("fot", [P, 1024], F32)
        yt_ = scE5.t("yt", [P, 1024], F32)
        sqscrF = scE5.t("sqscrF", [P, 1024], F32)
        sumsF = scE5.t("sumsF", [P, 8], F32)
        rstdF = scE5.t("rstdF", [P, 1], F32)
        nmrF = scE5.t("nmrF", [P, 1], F32)
        if True:
            westg = (westg0, westg1)
            webf = (webf0, webf1)

            acc_slices = [psA[:, 0:512], psA[:, 512:1024], psB[:, 0:512],
                          psB[:, 512:1024], psC[:, 0:512], psC[:, 512:1024],
                          psD[:, 0:512], psT[:, :].bitcast(F32)]
            we_free = [0, 0]
            webf_free = [0, 0]
            xeh_mm_done = 0
            xeh_war = 0
            accs_free = 0
            combT_free = 0
            xe_wr_done = {}
            cnat_done = 0
            fo_done = 0
            yadd_done = 0
            y_st_done = 0
            for half in range(2):
                tof = half * 1024
                gb_start = sch.c("vV")
                for e_i in range(NEXP):
                    for hcol in range(2):
                        w = []
                        if e_i == 0 and hcol == 0:
                            w = [("aA", accs_free), ("aA", cnat_done)]
                        sch.add("tensor", mm(psA[0:1, 512 * hcol:512 * (hcol + 1)],
                                             identf[0:32, e_i:e_i + 1],
                                             gateT[0:32, tof + 512 * hcol:tof + 512 * (hcol + 1)],
                                             start=True, stop=True),
                                waits=w, incs=((("tT", 1),) if hcol == 1 else ()))
                    tg = sch.c("tT")
                    sch.add("vector", lambda e: e.tensor_copy(selrowE[0:1, :], psA[0:1, 0:1024]),
                            waits=(("tT", tg),), incs=(("vV", 1),))
                    sg = sch.c("vV")
                    for hcol in range(2):
                        sch.add("tensor", lambda e, hcol=hcol: e.matmul(
                            psB[:, 512 * hcol:512 * (hcol + 1)], ones1[:, :],
                            selrowE[0:1, 512 * hcol:512 * (hcol + 1)],
                            start=True, stop=True),
                            waits=(("vV", sg),), incs=((("tT", 1),) if hcol == 1 else ()))
                    tg = sch.c("tT")
                    sch.add("vector", lambda e, e_i=e_i: e.tensor_copy(
                        gbce_all[:, 1024 * e_i:1024 * (e_i + 1)], psB[:, :]),
                        waits=(("tT", tg),), incs=(("vV", 1),))
                gb_ready = sch.c("vV")
                for dh in range(2):
                    for e_i in range(NEXP):
                        wbuf = e_i % 2
                        for cc in range(8):
                            w = []
                            if cc == 0:
                                w = [("vV", gb_ready), ("tT", xeh_mm_done)]
                            sch.add("vector", lambda e, cc=cc, tof=tof, e_i=e_i: e.tensor_tensor(
                                xeh[:, 1024 * cc:1024 * (cc + 1)],
                                h2T[:, TLOC * cc + tof:TLOC * cc + tof + 1024],
                                gbce_all[:, 1024 * e_i:1024 * (e_i + 1)], ALU.mult),
                                waits=w,
                                incs=((("vV", 1),) if cc == 7 else ()))
                        xel = ("vV", sch.c("vV"))
                        for pi, piece in enumerate(range(0, 8, 4)):
                            stg = westg[pi]
                            src = AP(f_we, e_i * (2 * DM * 512) + dh * (DM * 512)
                                     + piece * P * 512,
                                     [[512, P], [P * 512, 4], [1, 512]])
                            sch.add("sync", dma(stg[:, :], src),
                                    waits=(("aA", we_free[pi]),), incs=((f"dW{pi}", 16),))
                            dl = sch.c(f"dW{pi}")
                            sch.add("scalar", lambda e, s=stg, b=webf[wbuf],
                                    o=piece * 512, pi=pi: e.copy(b[:, o:o + 2048], s[:, :]),
                                    waits=((f"dW{pi}", dl), ("tT", webf_free[wbuf])),
                                    incs=(("aA", 1),))
                            we_free[pi] = sch.c("aA")
                        wew = sch.c("aA")
                        for cc in range(8):
                            for d4 in range(4):
                                for p2 in range(2):
                                    acc = acc_slices[d4 * 2 + p2]
                                    w = []
                                    if cc == 0 and d4 == 0 and p2 == 0:
                                        w = [xel, ("aA", wew)]
                                        if e_i == 0:
                                            w.append(("aA", accs_free))
                                            w.append(("vV", gb_ready))
                                    sch.add("tensor", mm(
                                        acc,
                                        webf[wbuf][:, 512 * cc + 128 * d4:512 * cc + 128 * (d4 + 1)],
                                        xeh[:, 1024 * cc + 512 * p2:1024 * cc + 512 * (p2 + 1)],
                                        start=(e_i == 0 and cc == 0),
                                        stop=(e_i == NEXP - 1 and cc == 7)),
                                        waits=w,
                                        incs=((("tT", 1),) if (cc == 7 and d4 == 3 and p2 == 1) else ()))
                        xeh_mm_done = sch.c("tT")
                        webf_free[wbuf] = sch.c("tT")
                    tw = sch.c("tT")
                    for d4 in range(4):
                        for p2 in range(2):
                            acc = acc_slices[d4 * 2 + p2]
                            o = (4 * dh + d4) * 1024 + 512 * p2
                            w = []
                            if d4 == 0 and p2 == 0:
                                w = [("tT", tw), ("tT", combT_free)]
                            sch.add("scalar", lambda e, a=acc, o=o: e.copy(
                                combT[:, o:o + 512], a),
                                waits=w,
                                incs=((("aA", 1),) if (d4 == 3 and p2 == 1) else ()))
                    accs_free = sch.c("aA")
                if debug:
                    sch.add("sync", dma(dbg_comb.ap()[half], combT[:, :]),
                            waits=(("aA", accs_free),), incs=(("dDbg", 16),))
                dbg_comb_tok = sch.c("dDbg")
                for tl in range(8):
                    tt = half * 8 + tl
                    for i in range(8):
                        w = []
                        if i == 0:
                            w = [("aA", accs_free), ("aA", cnat_done)]
                            if debug:
                                w.append(("dDbg", dbg_comb_tok))
                        sch.add("tensor", lambda e, i=i, tl=tl: e.transpose(
                            psC[:, 128 * i:128 * (i + 1)],
                            combT[:, 1024 * i + 128 * tl:1024 * i + 128 * (tl + 1)],
                            identf[:, :]),
                            waits=w, incs=((("tT", 1),) if i == 7 else ()))
                    tw = sch.c("tT")
                    if tl == 7:
                        combT_free = tw
                    sch.add("scalar", lambda e: e.copy(cnat[:, :], psC[:, 0:1024]),
                            waits=(("tT", tw), ("vV", fo_done)), incs=(("aA", 1),))
                    cnat_done = sch.c("aA")
                    src = AP(v1_s, 128 * tt * DM, [[DM, P], [1, DM]])
                    sch.add("gpsimd", dma(v1r[:, :], src),
                            waits=(("vV", yadd_done), v1_st_tok), incs=(("dG", 16),))
                    src = AP(x1_s, 128 * tt * DM, [[DM, P], [1, DM]])
                    sch.add("gpsimd", dma(x1r[:, :], src), incs=(("dG", 16),))
                    dl = sch.c("dG")
                    sch.add("vector", lambda e: e.tensor_tensor(
                        fot[:, :], cnat[:, :], v1r[:, :], ALU.add),
                        waits=(("dG", dl), ("aA", cnat_done)), incs=(("vV", 1),))
                    fo_done = sch.c("vV")
                    vw = emit_stats(fot[:, :], DM, EPS_LN, rstdF[:, 0:1], nmrF[:, 0:1],
                                    sumsF, sqscrF)
                    sch.add("vector", lambda e: e.tensor_scalar(
                        fot[:, :], fot[:, :], rstdF[:, 0:1], nmrF[:, 0:1],
                        ALU.mult, ALU.add), waits=(("vV", vw),))
                    sch.add("vector", lambda e: e.tensor_tensor(
                        yt_[:, :], fot[:, :], x1r[:, :], ALU.add),
                        waits=(("dZ0", y_st_done),), incs=(("vV", 1),))
                    yadd_done = sch.c("vV")
                    vw = sch.c("vV")
                    dst = AP(y_out, 128 * tt * DM, [[DM, P], [1, DM]])
                    sch.add("sync", dma(dst, yt_[:, :]), waits=(("vV", vw),),
                            incs=(("dZ0", 16),))
                    y_st_done = sch.c("dZ0")

        scE5.close()

    allds = [(nm, sch.cnt.get(nm, 0)) for nm in
             ("dC0", "dC1", "dW0", "dW1", "dX0", "dX1", "dY0", "dY1",
              "dF", "dR", "dH0", "dH1", "dZ0", "dE", "dG", "dB", "dDbg")]
    sch.add("sync", None, waits=tuple(allds), incs=(("ph", 1),))

    nc._sch_ref = sch
    semnames = sorted({s for e in ENGS for (w, _, ii) in sch.items[e]
                       for (s, _v) in list(w) + list(ii)})
    from contextlib import ExitStack
    with ExitStack() as stk:
        sems = {s: stk.enter_context(nc.semaphore(s)) for s in semnames}
        with nc.Block() as block:
            def runner(name):
                def run(eng):
                    for (w, fn, ii) in sch.items[name]:
                        for (s, v) in w:
                            eng.wait_ge(sems[s], v)
                        if fn is None:
                            for (s, n) in ii:
                                eng.sem_inc(sems[s], n)
                        else:
                            inst = fn(eng)
                            for (s, n) in ii:
                                inst = inst.then_inc(sems[s], n)
                return run
            block.sync(runner("sync"))
            block.tensor(runner("tensor"))
            block.vector(runner("vector"))
            block.scalar(runner("scalar"))
            block.gpsimd(runner("gpsimd"))
    return nc


def make_in_maps(inputs):
    x = np.ascontiguousarray(inputs["L0_attn_x"], dtype=np.float32)
    rel = np.asarray(inputs["L0_attn_rel_table"], dtype=np.float32)

    relT_full = np.empty((16, 2600), np.float32)
    relT_full[:, :2047] = rel.T
    relT_full[:, 2047:] = rel.T[:, 2046:2047]
    winf128 = np.ascontiguousarray(rel[2046, np.arange(P) % 16].reshape(P, 1))

    def bcast(v, n):
        return np.ascontiguousarray(np.broadcast_to(np.asarray(v, np.float32)[None, :], (P, n)))

    shared = {
        "a_wq": np.ascontiguousarray(inputs["L0_attn_Wq"], np.float32),
        "a_wk": np.ascontiguousarray(inputs["L0_attn_Wk"], np.float32),
        "a_wv": np.ascontiguousarray(
            np.stack([np.asarray(inputs["L0_attn_Wv"], np.float32)[:, :512],
                      np.asarray(inputs["L0_attn_Wv"], np.float32)[:, 512:]])),
        "a_gamma": np.ascontiguousarray(inputs["L0_attn_gamma"], np.float32),
        "a_beta": np.ascontiguousarray(inputs["L0_attn_beta"], np.float32),
        "winf128": winf128,
        "gammab_d": bcast(inputs["L0_attn_gamma"], 1024),
        "betab_d": bcast(inputs["L0_attn_beta"], 1024),
        "g1b_d": bcast(inputs["L0_ffn_gamma1"], 1024),
        "g2b_d": bcast(inputs["L0_ffn_gamma2"], 1024),
        "bqb_d": bcast(inputs["L0_ffn_bq"], 1024),
        "bdnb_d": bcast(inputs["L0_ffn_bdown"], 64),
        "bupb_d": bcast(inputs["L0_ffn_bup"], 2048),
        "bob_d": bcast(inputs["L0_ffn_bo"], 1024),
        "bgb_d": bcast(inputs["L0_ffn_bg"], 8),
        "f_g1": np.ascontiguousarray(inputs["L0_ffn_gamma1"], np.float32),
        "f_wq": np.ascontiguousarray(inputs["L0_ffn_Wq"], np.float32),
        "f_bq": np.ascontiguousarray(inputs["L0_ffn_bq"], np.float32),
        "f_wdn": np.ascontiguousarray(inputs["L0_ffn_Wdown"], np.float32),
        "f_bdn": np.ascontiguousarray(inputs["L0_ffn_bdown"], np.float32),
        "f_wup": np.ascontiguousarray(
            np.vstack([np.asarray(inputs["L0_ffn_Wup"], np.float32),
                       np.zeros((64, 2048), np.float32)])),
        "f_bup": np.ascontiguousarray(inputs["L0_ffn_bup"], np.float32),
        "f_wo": np.ascontiguousarray(inputs["L0_ffn_Wo"], np.float32),
        "f_bo": np.ascontiguousarray(inputs["L0_ffn_bo"], np.float32),
        "f_g2": np.ascontiguousarray(inputs["L0_ffn_gamma2"], np.float32),
        "f_wg": np.ascontiguousarray(inputs["L0_ffn_Wg"], np.float32),
        "f_bg": np.ascontiguousarray(inputs["L0_ffn_bg"], np.float32),
        "f_we": np.ascontiguousarray(
            np.stack([np.stack([w[:, :512], w[:, 512:]])
                      for w in np.asarray(inputs["L0_ffn_We"], np.float32)])),
        "identf_d": np.eye(P, dtype=np.float32),
        "identb_d": np.eye(P).astype(ml_dtypes.bfloat16),
    }

    in_maps = []
    for cid in range(8):
        b, g = cid // 4, cid % 4
        rw = np.empty((len(DPAIRS), P, WINW), np.float32)
        parr = np.arange(P)
        for pidx, (h, j) in enumerate(DPAIRS):
            K = S - 1 - 512 * h + 512 * j
            c_base = K - 128 * g - 127
            for c in range(WINW):
                rw[pidx, :, c] = relT_full[parr % 16, c_base + c + parr // 16]
        m = dict(shared)
        m["x_b"] = x[b]
        m["x_loc"] = np.ascontiguousarray(x[b, TLOC * g:TLOC * (g + 1)])
        m["relwin"] = rw
        in_maps.append(m)

    return in_maps


def kernel(**inputs):
    in_maps = make_in_maps(inputs)
    nc = build_program()
    res = run_bass_kernel_spmd(nc, in_maps, core_ids=list(range(8)))
    y = np.empty((2, S, DM), np.float32)
    for cid in range(8):
        b, g = cid // 4, cid % 4
        y[b, TLOC * g:TLOC * (g + 1)] = res.results[cid]["y"]
    return y
